# version 02
# baseline (speedup 1.0000x reference)
import numpy as np
import ml_dtypes
import contextlib
import concourse.bass as bass
import concourse.mybir as mybir
from concourse.bass_utils import run_bass_kernel_spmd
from concourse.alu_op_type import AluOpType

F32 = mybir.dt.float32
BF16 = mybir.dt.bfloat16
FP8 = mybir.dt.float8e4
ACT = mybir.ActivationFunctionType
AX = mybir.AxisListType
bf16 = ml_dtypes.bfloat16
f8 = ml_dtypes.float8_e4m3fn

B, T, D = 4, 4096, 1024
TH = 2048
EXT = 2176
NC = 8

EYE = np.eye(128, dtype=np.float32)
TWOI = 2.0 * EYE
CUMU = np.triu(np.ones((128, 128), np.float32))
STRICTL = np.tril(np.ones((128, 128), np.float32), -1)
INCU = np.triu(np.ones((128, 128), np.float32))
ONES1 = np.ones((1, 128), np.float32)
ONES128 = np.ones((128, 1), np.float32)
ONES128B = np.ones((128, 1), np.float32).astype(bf16)
G64S = np.zeros((128, 2), np.float32)
for _p in range(128):
    G64S[_p, _p // 64] = 1.0
G64M = G64S / 64.0
H2 = G64S.T.copy()
P8 = np.zeros((8, 128), np.float32)
for _p in range(128):
    P8[_p % 8, _p] = 1.0
P16 = np.zeros((16, 128), np.float32)
for _p in range(128):
    P16[_p // 8, _p] = 1.0
Q8 = P8.T.copy()
E127 = np.zeros((128, 1), np.float32)
E127[127, 0] = 1.0
E8REP = np.zeros((8, 8 * 128), np.float32)
for _e in range(8):
    E8REP[_e, _e * 128:(_e + 1) * 128] = 1.0
G64SELS = np.zeros((128, 128), np.float32)
for _mc in range(8):
    for _p in range(128):
        G64SELS[_p, _mc * 16 + 2 * _mc + _p // 64] = 1.0
G64SELM = G64SELS / 64.0
H2SEL = np.zeros((16, 8 * 128), np.float32)
for _mc in range(8):
    for _p in range(128):
        H2SEL[2 * _mc + _p // 64, _mc * 128 + _p] = 1.0

_PROG = {}


class K:

    def __init__(self, name):
        self.name = name
        nc = bass.Bass("TRN2", target_bir_lowering=False)
        self.nc = nc
        self.stack = contextlib.ExitStack()
        self.S = self.stack.enter_context(nc.semaphore("S"))
        self.n = 0
        self.outs = []

    def inp(self, name, shape, dtype=F32):
        return self.nc.dram_tensor(name, list(shape), dtype, kind="ExternalInput").ap()

    def out(self, name, shape, dtype=F32):
        self.outs.append(name)
        return self.nc.dram_tensor(name, list(shape), dtype, kind="ExternalOutput").ap()

    def sb(self, name, shape, dtype=F32):
        return self.stack.enter_context(self.nc.sbuf_tensor(name, list(shape), dtype))

    def ps(self, name, shape, dtype=F32):
        return self.stack.enter_context(self.nc.psum_tensor(name, list(shape), dtype))

    def _w(self, eng):
        eng.wait_ge(self.S, self.n)

    def _i(self, instr, inc=1):
        instr.then_inc(self.S, inc)
        self.n += inc

    def dma(self, out, in_, **kw):
        e = self.nc.sync
        self._w(e)
        self._i(e.dma_start(out=out, in_=in_, **kw), 16)

    def v(self, opname_, *a, **kw):
        self._w(self.nc.vector)
        self._i(getattr(self.nc.vector, opname_)(*a, **kw))

    def s(self, opname_, *a, **kw):
        self._w(self.nc.scalar)
        self._i(getattr(self.nc.scalar, opname_)(*a, **kw))

    def mm(self, ops):
        t = self.nc.tensor
        self._w(t)
        last = None
        for o in ops:
            last = t.matmul(**o)
        self._i(last)

    def tr(self, triples):
        t = self.nc.tensor
        self._w(t)
        last = None
        for (o, i_, idn) in triples:
            last = t.transpose(out=o, in_=i_, identity=idn)
        self._i(last)

    def done(self):
        self.nc.gpsimd.wait_ge(self.S, self.n)
        self.stack.close()



def consts(k, names=("eye", "ones1")):
    c = {}
    host = {"eye": EYE, "twoi": TWOI, "cumu": CUMU, "strictl": STRICTL,
            "incu": INCU, "ones1": ONES1, "ones128": ONES128, "ones128b": ONES128B,
            "g64s": G64S, "g64m": G64M, "h2": H2, "p8": P8, "p16": P16, "q8": Q8,
            "g64sels": G64SELS, "g64selm": G64SELM, "h2sel": H2SEL, "e127": E127, "e8rep": E8REP}
    for nm in names:
        a = host[nm]
        dt = BF16 if a.dtype == bf16 else F32
        ap = k.inp("c_" + nm, list(a.shape), dt)
        sb = k.sb("cs_" + nm, list(a.shape), dt)
        k.dma(sb[:], ap)
        c[nm] = sb
    return c


def const_maps():
    return {"c_eye": EYE, "c_twoi": TWOI, "c_cumu": CUMU, "c_strictl": STRICTL,
            "c_incu": INCU, "c_ones1": ONES1, "c_ones128": ONES128,
            "c_ones128b": ONES128B, "c_g64s": G64S, "c_g64m": G64M, "c_h2": H2,
            "c_p8": P8, "c_p16": P16, "c_q8": Q8, "c_g64sels": G64SELS,
            "c_g64selm": G64SELM, "c_h2sel": H2SEL, "c_e127": E127, "c_e8rep": E8REP}


def mm_fm(k, lhsT, rhs, KC, MO, TN, evac, psA, psB, tN=512, mN=128):
    nk = KC // 128
    pp = 0
    for m0 in range(0, MO, mN):
        mw = min(mN, MO - m0)
        for t0 in range(0, TN, tN):
            tw = min(tN, TN - t0)
            psb = (psA, psB)[pp % 2]
            pp += 1
            ops = [dict(out=psb[0:mw, 0:tw], lhsT=lhsT(kc, m0, mw), rhs=rhs(kc, t0, tw),
                        start=(kc == 0), stop=(kc == nk - 1)) for kc in range(nk)]
            k.mm(ops)
            evac(m0, mw, t0, tw, psb[0:mw, 0:tw])


def load_w(k, hbm, wbf, KC, MO, stage):
    nk = KC // 128
    for kc in range(nk):
        k.dma(stage[:, 0:MO], hbm[kc * 128:(kc + 1) * 128, :])
        k.v('tensor_copy', out=wbf[:, kc * MO:(kc + 1) * MO], in_=stage[:, 0:MO])


def repl_row(k, row_ap, width, dst, c, ps, dt=F32):
    for t0 in range(0, width, 512):
        tw = min(512, width - t0)
        k.mm([dict(out=ps[0:128, 0:tw], lhsT=c["ones1"][:], rhs=row_ap[:, t0:t0 + tw],
                   start=True, stop=True)])
        k.v('tensor_copy', out=dst[:, t0:t0 + tw], in_=ps[0:128, 0:tw])


def ln_fm(k, buf, TN, c, psA, psB, rowbuf, repbuf, affine=None, eps=1e-5, rms=False,
          out_cb=None, load_tile=None, load_chunk=None):
    nt = TN // 128
    if not hasattr(k, "_lnsq"):
        k._lnsq = k.sb("lnsq", [128, 128], F32)
        k._lntmp = k.sb("lntmp", [128, 2048], F32)
    sq = k._lnsq
    stat = k.sb(f"lnst_{k.n}", [128, nt * 2], F32)
    cht = None
    if buf is None:
        cht = k.sb(f"lnch_{k.n}", [128, TN], F32)

    sq2 = k.sb(f"lnt2_{k.n}", [128, 8 * 128], F32) if buf is None else None
    if not hasattr(k, "_lnsqb"):
        k._lnsqb = k.sb("lnsqb", [128, 8 * 128], F32)
    sqb = k._lnsqb

    def tile(kc, t0):
        if buf is not None:
            return buf[:, kc * TN + t0: kc * TN + t0 + 128]
        load_tile(kc, t0, 128, sq2[:, kc * 128:(kc + 1) * 128])
        return sq2[:, kc * 128:(kc + 1) * 128]
    for tt in range(nt):
        t0 = tt * 128
        tiles = []
        for kc in range(8):
            src_ = tile(kc, t0)
            tiles.append(src_)
            k.s('activation', out=sqb[:, kc * 128:(kc + 1) * 128], in_=src_,
                func=ACT.Square)
        if not rms:
            k.mm([dict(out=psA[0:128, 0:1], lhsT=tiles[kc], rhs=c["ones128"][:],
                       start=(kc == 0), stop=(kc == 7)) for kc in range(8)])
        k.mm([dict(out=psB[0:128, 0:1], lhsT=sqb[:, kc * 128:(kc + 1) * 128],
                   rhs=c["ones128"][:], start=(kc == 0), stop=(kc == 7))
              for kc in range(8)])
        if rms:
            k.v('tensor_scalar', out=stat[:, 2 * tt:2 * tt + 1], in0=psB[0:128, 0:1],
                scalar1=1.0 / D, scalar2=None, op0=AluOpType.mult)
            k.v('tensor_scalar', out=stat[:, 2 * tt:2 * tt + 1],
                in0=stat[:, 2 * tt:2 * tt + 1], scalar1=eps, scalar2=None,
                op0=AluOpType.add)
            k.s('activation', out=stat[:, 2 * tt:2 * tt + 1],
                in_=stat[:, 2 * tt:2 * tt + 1], func=ACT.Sqrt)
            k.v('reciprocal', out=stat[:, 2 * tt:2 * tt + 1],
                in_=stat[:, 2 * tt:2 * tt + 1])
            k.v('memset', stat[:, 2 * tt + 1:2 * tt + 2], 0.0)
        else:
            mu = stat[:, 2 * tt + 1:2 * tt + 2]
            k.v('tensor_scalar', out=mu, in0=psA[0:128, 0:1], scalar1=1.0 / D,
                scalar2=None, op0=AluOpType.mult)
            ms = stat[:, 2 * tt:2 * tt + 1]
            k.v('tensor_scalar', out=ms, in0=psB[0:128, 0:1], scalar1=1.0 / D,
                scalar2=None, op0=AluOpType.mult)
            k.s('activation', out=sq[:, 0:1], in_=mu, func=ACT.Square)
            k.v('tensor_tensor', out=ms, in0=ms, in1=sq[:, 0:1], op=AluOpType.subtract)
            k.v('tensor_scalar', out=ms, in0=ms, scalar1=eps, scalar2=None,
                op0=AluOpType.add)
            k.s('activation', out=ms, in_=ms, func=ACT.Sqrt)
            k.v('reciprocal', out=ms, in_=ms)
            k.v('tensor_tensor', out=mu, in0=mu, in1=ms, op=AluOpType.mult)
            k.v('tensor_scalar', out=mu, in0=mu, scalar1=-1.0, scalar2=None,
                op0=AluOpType.mult)
    pst_ = k.ps(f"lnpt_{k.n}", [128, 256], F32)
    if not hasattr(k, "_lnrow"):
        k._lnrow = k.sb("lnrow", [1, 2 * 2176], F32)
    rows = k._lnrow
    for tt in range(nt):
        k.tr([(pst_[0:1, 0:128], stat[:, 2 * tt:2 * tt + 1], c["eye"][:]),
              (pst_[0:1, 128:256], stat[:, 2 * tt + 1:2 * tt + 2], c["eye"][:])])
        k.v('tensor_copy', out=rows[0:1, tt * 128:(tt + 1) * 128], in_=pst_[0:1, 0:128])
        k.v('tensor_copy', out=rows[0:1, TN + tt * 128:TN + (tt + 1) * 128],
            in_=pst_[0:1, 128:256])
    repl_row(k, rows[0:1, 0:TN], TN, repbuf[:, 0:TN], c, psA)
    repl_row(k, rows[0:1, TN:2 * TN], TN, repbuf[:, TN:2 * TN], c, psA)
    tmp = k._lntmp
    for kc in range(8):
        if buf is not None:
            sl = buf[:, kc * TN:(kc + 1) * TN]
        else:
            load_chunk(kc, cht[:])
            sl = cht[:]
        for t0 in range(0, TN, 2048):
            tw = min(2048, TN - t0)
            k.v('tensor_tensor', out=tmp[:, 0:tw], in0=sl[:, t0:t0 + tw],
                in1=repbuf[:, t0:t0 + tw], op=AluOpType.mult)
            k.v('tensor_tensor', out=sl[:, t0:t0 + tw], in0=tmp[:, 0:tw],
                in1=repbuf[:, TN + t0:TN + t0 + tw], op=AluOpType.add)
        if affine is not None:
            w_col, b_col = affine
            k.v('tensor_scalar', out=sl, in0=sl, scalar1=w_col(kc), scalar2=None,
                op0=AluOpType.mult)
            k.v('tensor_scalar', out=sl, in0=sl, scalar1=b_col(kc), scalar2=None,
                op0=AluOpType.add)
        if out_cb is not None:
            out_cb(kc, sl)


def scan_seg(k, out, d0, d1, TN, seg=512, init0=0.0):
    t0 = 0
    first = True
    while t0 < TN:
        tw = min(seg, TN - t0)
        init = init0 if first else out[:, t0 - 1:t0]
        k.v('tensor_tensor_scan', out=out[:, t0:t0 + tw], data0=d0[:, t0:t0 + tw],
            data1=d1[:, t0:t0 + tw], initial=init, op0=AluOpType.mult,
            op1=AluOpType.add)
        first = False
        t0 += tw


def softplus(k, dst, src, t1):
    k.s('activation', out=t1, in_=src, func=ACT.Abs)
    k.v('tensor_scalar', out=t1, in0=t1, scalar1=-1.0, scalar2=None, op0=AluOpType.mult)
    k.s('activation', out=t1, in_=t1, func=ACT.Exp)
    k.v('tensor_scalar', out=t1, in0=t1, scalar1=1.0, scalar2=None, op0=AluOpType.add)
    k.s('activation', out=t1, in_=t1, func=ACT.Ln)
    k.v('tensor_scalar', out=dst, in0=src, scalar1=0.0, scalar2=None, op0=AluOpType.max)
    k.v('tensor_tensor', out=dst, in0=dst, in1=t1, op=AluOpType.add)


def silu(k, dst, src, tmp):
    k.s('activation', out=tmp, in_=src, func=ACT.Sigmoid)
    k.v('tensor_tensor', out=dst, in0=src, in1=tmp, op=AluOpType.mult)



def build_pre_a():
    k = K("pre_a")
    c = consts(k, ("eye", "ones1", "ones128"))
    xext = k.inp("xext", [D, EXT])
    wipT = k.inp("wipT", [D, D])
    cwc = k.inp("cwc", [128, 8 * 4])
    cbc = k.inp("cbc", [128, 8])
    f8k = k.inp("f8k", [D, D], BF16)
    s_in = k.inp("s_in", [1, 1])
    s_k = k.inp("s_k", [1, 1])
    fbc = k.inp("fbc", [128, 8], BF16)
    xp_o = k.out("xp", [D, TH], BF16)
    ffn_o = k.out("ffn0", [D, TH], BF16)
    xn_o = k.out("xn0", [D, TH], BF16)
    cws = k.sb("cws", [128, 8 * 4], F32)
    cbs = k.sb("cbs", [128, 8], F32)
    fbs = k.sb("fbs", [128, 8], BF16)
    fbsf = k.sb("fbsf", [128, 8], F32)
    k.dma(cws[:], cwc)
    k.dma(cbs[:], cbc)
    k.dma(fbs[:], fbc)
    k.v('tensor_copy', out=fbsf[:], in_=fbs[:])

    psA = k.ps("psA", [128, 512], F32)
    psB = k.ps("psB", [128, 512], F32)
    xnbf = k.sb("xnbf", [128, 8 * EXT], BF16)
    rowb = k.sb("rowb", [2, EXT], F32)
    repb = k.sb("repb", [128, 2 * EXT], F32)
    stage = k.sb("stage", [128, 1024], F32)
    wbf = k.sb("wbf", [128, 8 * 1024], BF16)
    xpch = k.sb("xpch", [128, EXT], F32)
    tmp = k.sb("tmp", [128, TH], F32)
    tmp2 = k.sb("tmp2", [128, TH], F32)
    tmpb = k.sb("tmpb", [128, TH], BF16)

    def lt(kc, t0, tw, dst):
        k.dma(dst, xext[kc * 128:(kc + 1) * 128, t0:t0 + tw])

    def lc(kc, dst):
        k.dma(dst, xext[kc * 128:(kc + 1) * 128, :])

    def xn_cb(kc, sl):
        k.v('tensor_copy', out=xnbf[:, kc * EXT:(kc + 1) * EXT], in_=sl)
        k.dma(xn_o[kc * 128:(kc + 1) * 128, :], xnbf[:, kc * EXT + 128:(kc + 1) * EXT])
    ln_fm(k, None, EXT, c, psA, psB, rowb, repb, out_cb=xn_cb, load_tile=lt,
          load_chunk=lc)

    load_w(k, wipT, wbf, D, D, stage)

    def ev_xp(m0, mw, t0, tw, ps):
        kc = m0 // 128
        k.v('tensor_copy', out=xpch[:, t0:t0 + tw], in_=ps)
        if t0 + tw == EXT:
            base = 125
            k.v('tensor_scalar', out=tmp[:], in0=xpch[:, base:base + TH],
                scalar1=cws[:, kc * 4:kc * 4 + 1], scalar2=None, op0=AluOpType.mult)
            for j in range(1, 4):
                k.v('scalar_tensor_tensor', out=tmp[:],
                    in0=xpch[:, base + j:base + j + TH],
                    scalar=cws[:, kc * 4 + j:kc * 4 + j + 1], in1=tmp[:],
                    op0=AluOpType.mult, op1=AluOpType.add)
            k.v('tensor_scalar', out=tmp[:], in0=tmp[:], scalar1=cbs[:, kc:kc + 1],
                scalar2=None, op0=AluOpType.add)
            silu(k, tmp[:], tmp[:], tmp2[:])
            k.v('tensor_copy', out=tmpb[:], in_=tmp[:])
            k.dma(xp_o[kc * 128:(kc + 1) * 128, :], tmpb[:])
    mm_fm(k, lambda kc, m0, mw: wbf[:, kc * D + m0:kc * D + m0 + mw],
          lambda kc, t0, tw: xnbf[:, kc * EXT + t0:kc * EXT + t0 + tw],
          D, D, EXT, ev_xp, psA, psB)

    sc = k.sb("sc", [1, 4], F32)
    k.dma(sc[0:1, 0:1], s_in)
    k.dma(sc[0:1, 1:2], s_k)
    k.v('tensor_tensor', out=sc[0:1, 2:3], in0=sc[0:1, 0:1], in1=sc[0:1, 1:2],
        op=AluOpType.mult)
    k.v('reciprocal', out=sc[0:1, 0:1], in_=sc[0:1, 0:1])
    k.v('reciprocal', out=sc[0:1, 1:2], in_=sc[0:1, 1:2])
    scr = k.sb("scr", [128, 4], F32)
    k.mm([dict(out=psA[0:128, 0:4], lhsT=c["ones1"][:], rhs=sc[0:1, 0:4],
               start=True, stop=True)])
    k.v('tensor_copy', out=scr[:], in_=psA[0:128, 0:4])
    xq = k.sb("xq", [128, 8 * TH], FP8)
    kq = k.sb("kq", [128, 8 * D], FP8)
    kbf = k.sb("kbf", [128, D], BF16)
    for kc in range(8):
        k.dma(kbf[:], f8k[kc * 128:(kc + 1) * 128, :])
        k.v('tensor_scalar', out=kq[:, kc * D:(kc + 1) * D], in0=kbf[:],
            scalar1=scr[:, 1:2], scalar2=None, op0=AluOpType.mult)
        k.v('tensor_scalar', out=xq[:, kc * TH:(kc + 1) * TH],
            in0=xnbf[:, kc * EXT + 128:(kc + 1) * EXT], scalar1=scr[:, 0:1],
            scalar2=None, op0=AluOpType.mult)

    def ev_ffn(m0, mw, t0, tw, ps):
        k.v('tensor_scalar', out=tmp[:, 0:tw], in0=ps, scalar1=scr[:, 2:3],
            scalar2=None, op0=AluOpType.mult)
        k.v('tensor_copy', out=tmpb[:, 0:tw], in_=tmp[:, 0:tw])
        k.v('tensor_scalar', out=tmpb[:, 0:tw], in0=tmpb[:, 0:tw],
            scalar1=fbsf[:, m0 // 128:m0 // 128 + 1], scalar2=None, op0=AluOpType.add)
        k.dma(ffn_o[m0:m0 + mw, t0:t0 + tw], tmpb[0:mw, 0:tw])
    mm_fm(k, lambda kc, m0, mw: kq[:, kc * D + m0:kc * D + m0 + mw],
          lambda kc, t0, tw: xq[:, kc * TH + t0:kc * TH + t0 + tw],
          D, D, TH, ev_ffn, psA, psB)
    k.done()
    return k


def build_pre_b():
    k = K("pre_b")
    c = consts(k, ("eye", "ones1", "g64sels", "g64selm", "h2sel"))
    xp_i = k.inp("xp", [D, TH], BF16)
    w = {nm: k.inp(nm, [D, D]) for nm in ("wqT", "wkT", "wvT", "waT", "wbT")}
    bac = k.inp("bac", [128, 8])
    bbc = k.inp("bbc", [128, 8])
    q_o = k.out("q", [D, TH], BF16)
    kk_o = k.out("k", [D, TH], BF16)
    v_o = k.out("v", [D, TH], BF16)
    la_o = k.out("loga", [16, TH])
    be_o = k.out("beta", [16, TH])

    psA = k.ps("psA", [128, 512], F32)
    psB = k.ps("psB", [128, 512], F32)
    ps2 = k.ps("ps2", [16, 512], F32)
    bas = k.sb("bas", [128, 8], F32)
    bbs = k.sb("bbs", [128, 8], F32)
    k.dma(bas[:], bac)
    k.dma(bbs[:], bbc)
    xp = k.sb("xp_s", [128, 8 * TH], BF16)
    stage = k.sb("stage", [128, 1024], F32)
    wbf = k.sb("wbf", [128, 8 * D], BF16)
    raw = k.sb("raw", [128, 8 * TH], F32)
    nrm = k.sb("nrm", [16, TH], F32)
    tmp = k.sb("tmp", [128, 512], F32)
    tmpb = k.sb("tmpb", [128, TH], BF16)
    for kc in range(8):
        k.dma(xp[:, kc * TH:(kc + 1) * TH], xp_i[kc * 128:(kc + 1) * 128, :])

    tmp8 = k.sb("tmp8", [128, 8 * 512], F32)

    def gemm_heads(ev, sel):
        for t0 in range(0, TH, 512):
            for mc in range(8):
                k.mm([dict(out=psA[0:128, 0:512],
                           lhsT=wbf[:, kc * D + mc * 128:kc * D + (mc + 1) * 128],
                           rhs=xp[:, kc * TH + t0:kc * TH + t0 + 512],
                           start=(kc == 0), stop=(kc == 7)) for kc in range(8)])
                ev(mc, t0, 512, psA[0:128, 0:512])
            k.mm([dict(out=ps2[0:16, 0:512], lhsT=sel[:, mc * 16:(mc + 1) * 16],
                       rhs=tmp8[:, mc * 512:(mc + 1) * 512],
                       start=(mc == 0), stop=(mc == 7)) for mc in range(8)])
            k.v('tensor_copy', out=nrm[0:16, t0:t0 + 512], in_=ps2[0:16, 0:512])

    for bc, oo in ((bas, la_o), (bbs, be_o)):
        nm = "waT" if oo is la_o else "wbT"
        load_w(k, w[nm], wbf, D, D, stage)

        def ev_ab(mc, t0, tw, ps, bc=bc):
            k.s('activation', out=tmp8[:, mc * 512:mc * 512 + tw], in_=ps,
                func=ACT.Sigmoid, bias=bc[:, mc:mc + 1])
        gemm_heads(ev_ab, c["g64selm"])
        if oo is la_o:
            k.s('activation', out=nrm[:], in_=nrm[:], func=ACT.Ln)
        k.dma(oo, nrm[0:16, :])

    for nm, oo in (("wqT", q_o), ("wkT", kk_o)):
        load_w(k, w[nm], wbf, D, D, stage)

        def ev_q(mc, t0, tw, ps):
            k.v('tensor_copy', out=raw[:, mc * TH + t0:mc * TH + t0 + tw], in_=ps)
            k.s('activation', out=tmp8[:, mc * 512:mc * 512 + tw], in_=ps,
                func=ACT.Square)
        gemm_heads(ev_q, c["g64sels"])
        k.s('activation', out=nrm[:], in_=nrm[:], func=ACT.Sqrt)
        k.v('tensor_scalar', out=nrm[:], in0=nrm[:], scalar1=1e-12, scalar2=None,
            op0=AluOpType.max)
        k.v('reciprocal', out=nrm[:], in_=nrm[:])
        for mc in range(8):
            for t0 in range(0, TH, 512):
                k.mm([dict(out=psA[0:128, 0:512],
                           lhsT=c["h2sel"][:, mc * 128:(mc + 1) * 128],
                           rhs=nrm[0:16, t0:t0 + 512], start=True, stop=True)])
                k.v('tensor_tensor', out=tmpb[:, t0:t0 + 512],
                    in0=raw[:, mc * TH + t0:mc * TH + t0 + 512], in1=psA[0:128, 0:512],
                    op=AluOpType.mult)
            k.dma(oo[mc * 128:(mc + 1) * 128, :], tmpb[:])
    load_w(k, w["wvT"], wbf, D, D, stage)

    def ev_v(m0, mw, t0, tw, ps):
        k.v('tensor_copy', out=tmpb[:, t0:t0 + tw], in_=ps)
        if t0 + tw == TH:
            k.dma(v_o[m0:m0 + 128, :], tmpb[:])
    mm_fm(k, lambda kc, m0, mw: wbf[:, kc * D + m0:kc * D + m0 + mw],
          lambda kc, t0, tw: xp[:, kc * TH + t0:kc * TH + t0 + tw],
          D, D, TH, ev_v, psA, psB)
    k.done()
    return k


def build_scan0():
    k = K("scan0")
    c = consts(k, ("eye", "twoi", "cumu", "strictl", "incu", "ones1", "e127"))
    CH, NCH, NP = 128, 32, 4
    qem_i = k.inp("qem", [64, 8 * T], BF16)
    kem_i = k.inp("kem", [64, 8 * T], BF16)
    ktm_i = k.inp("ktm", [128, 8 * NCH * 64], BF16)
    vtm_i = k.inp("vtm", [128, 8 * NCH * 64], BF16)
    la_i = k.inp("la", [128, 8 * NCH])
    be_i = k.inp("be", [128, 8 * NCH])
    cell_o = k.out("cell", [8, T, 64])

    qem = k.sb("qem_s", [64, NP * T], BF16)
    kem = k.sb("kem_s", [64, NP * T], BF16)
    ktm = k.sb("ktm_s", [128, NP * NCH * 64], BF16)
    vtm = k.sb("vtm_s", [128, NP * NCH * 64], BF16)
    la = k.sb("la_s", [128, NP * NCH], F32)
    be = k.sb("be_s", [128, NP * NCH], F32)

    lg = k.sb("lg_s", [128, NP * NCH], F32)
    gam = k.sb("gam", [128, NP * NCH], F32)
    gr = k.sb("gr", [128, NP * NCH], F32)
    psA = k.ps("pA", [128, 1024], F32)
    psB = k.ps("pB", [128, 1024], F32)
    psC = k.ps("pC", [128, 512], F32)
    psD = k.ps("pD", [128, 512], F32)
    psE = k.ps("pE", [64, 512], F32)

    def pst(j):
        return psA[0:128, j * CH:(j + 1) * CH] if j < 4 else \
            psB[0:128, (j - 4) * CH:(j - 3) * CH]

    def colv(t, cc, w=1):
        return t[:].rearrange("p (j cc) -> p j cc", cc=NCH)[:, :, cc:cc + 1]

    S = k.sb("S", [64, NP * 64], F32)
    Sbf = k.sb("Sbf", [64, NP * 64], BF16)
    d1 = k.sb("d1", [128, NP * CH], F32)
    ratio = k.sb("ratio", [128, NP * CH], F32)
    ratioT = k.sb("ratioT", [128, NP * CH], F32)
    kksb = k.sb("kksb", [128, NP * CH], F32)
    kqsb = k.sb("kqsb", [128, NP * CH], F32)
    Lm = k.sb("Lm", [128, NP * CH], F32)
    Lt = k.sb("Lt", [128, NP * CH], F32)
    X = k.sb("X", [128, NP * CH], F32)
    Xt = k.sb("Xt", [128, NP * CH], F32)
    M = k.sb("M", [128, NP * CH], F32)
    AqT = k.sb("AqT", [128, NP * CH], F32)
    RHS = k.sb("RHS", [128, NP * 64], F32)
    Wsb = k.sb("Wsb", [128, NP * 64], F32)
    Wbf = k.sb("Wbf", [128, NP * 64], BF16)
    Osb = k.sb("Osb", [128, NP * 64], F32)
    Vf = k.sb("Vf", [128, NP * 64], F32)
    gc = k.sb("gc", [64, NP], F32)
    lgr = k.sb("lgr", [1, 512], F32)
    gc0 = k.sb("gc0", [1, 128], F32)
    lgcol8 = k.sb("lgcol8", [128, 4], F32)
    tt2 = k.sb("tt2", [128, NP * CH], F32)

    eyeb = c["eye"][:].unsqueeze(1).broadcast_to([128, NP, CH])
    twob = c["twoi"][:].unsqueeze(1).broadcast_to([128, NP, CH])
    stlb = c["strictl"][:].unsqueeze(1).broadcast_to([128, NP, CH])
    incb = c["incu"][:].unsqueeze(1).broadcast_to([128, NP, CH])

    def v3(t):
        return t[:].rearrange("p (j f) -> p j f", f=CH)

    def v64(t):
        return t[:].rearrange("p (j f) -> p j f", f=64)

    for half in range(2):
        HO = half * NP
        for src_, dst, w_ in ((qem_i, qem, T), (kem_i, kem, T),
                              (ktm_i, ktm, NCH * 64), (vtm_i, vtm, NCH * 64),
                              (la_i, la, NCH), (be_i, be, NCH)):
            k.dma(dst[:], src_[:, HO * w_:(HO + NP) * w_])
        k.mm([dict(out=psA[0:128, 0:128], lhsT=c["cumu"][:], rhs=la[:, 0:128],
                   start=True, stop=True)])
        k.v('tensor_copy', out=lg[:, 0:128], in_=psA[0:128, 0:128])
        k.s('activation', out=gam[:], in_=lg[:], func=ACT.Exp)
        k.mm([dict(out=psC[0:1, 0:128], lhsT=c["e127"][:], rhs=lg[:, 0:128],
                   start=True, stop=True)])
        k.v('tensor_copy', out=lgr[0:1, 0:128], in_=psC[0:1, 0:128])
        k.mm([dict(out=psA[0:128, 0:128], lhsT=c["ones1"][:], rhs=lgr[0:1, 0:128],
                   start=True, stop=True)])
        k.v('tensor_tensor', out=gr[:, 0:128], in0=psA[0:128, 0:128],
            in1=lg[:, 0:128], op=AluOpType.subtract)
        k.s('activation', out=gr[:], in_=gr[:], func=ACT.Exp)
        k.v('tensor_tensor', out=gc0[0:1, :], in0=gr[0:1, :], in1=gam[0:1, :],
            op=AluOpType.mult)
        k.v('memset', S[:], 0.0)
        for cc in range(NCH):
            lgc = colv(lg, cc).broadcast_to([128, NP, CH])
            bec = colv(be, cc).broadcast_to([128, NP, CH])
            bec64 = colv(be, cc).broadcast_to([128, NP, 64])
            gac = colv(gam, cc).broadcast_to([128, NP, 64])
            grc = colv(gr, cc).broadcast_to([128, NP, 64])
            k.v('tensor_copy', out=lgcol8[:],
                in_=colv(lg, cc).rearrange("p j o -> p (j o)"))
            k.tr([(psC[0:1, j * 128:(j + 1) * 128], lgcol8[:, j:j + 1], c["eye"][:])
                  for j in range(NP)])
            k.v('tensor_copy', out=lgr[0:1, :], in_=psC[0:1, 0:512])
            k.mm([dict(out=pst(j), lhsT=c["ones1"][:],
                       rhs=lgr[0:1, j * 128:(j + 1) * 128],
                       start=True, stop=True) for j in range(NP)])
            for j in range(NP):
                k.v('tensor_tensor', out=v3(d1)[:, j, :], in0=lgc[:, j, :], in1=pst(j),
                    op=AluOpType.subtract)
            k.v('tensor_scalar', out=ratio[:], in0=d1[:], scalar1=0.0, scalar2=None,
                op0=AluOpType.min)
            k.s('activation', out=ratio[:], in_=ratio[:], func=ACT.Exp)
            k.v('tensor_scalar', out=ratioT[:], in0=d1[:], scalar1=-1.0, scalar2=0.0,
                op0=AluOpType.mult, op1=AluOpType.min)
            k.s('activation', out=ratioT[:], in_=ratioT[:], func=ACT.Exp)
            k.mm([dict(out=pst(j), lhsT=kem[:, j * T + cc * CH:j * T + (cc + 1) * CH],
                       rhs=kem[:, j * T + cc * CH:j * T + (cc + 1) * CH],
                       start=True, stop=True) for j in range(NP)])
            for j in range(NP):
                k.v('tensor_copy', out=kksb[:, j * CH:(j + 1) * CH], in_=pst(j))
            k.mm([dict(out=pst(j), lhsT=kem[:, j * T + cc * CH:j * T + (cc + 1) * CH],
                       rhs=qem[:, j * T + cc * CH:j * T + (cc + 1) * CH],
                       start=True, stop=True) for j in range(NP)])
            for j in range(NP):
                k.v('tensor_copy', out=kqsb[:, j * CH:(j + 1) * CH], in_=pst(j))
            k.v('tensor_tensor', out=Lm[:], in0=ratio[:], in1=kksb[:], op=AluOpType.mult)
            k.v('tensor_tensor', out=v3(Lm), in0=v3(Lm), in1=bec, op=AluOpType.mult)
            k.v('tensor_tensor', out=v3(Lm), in0=v3(Lm), in1=stlb, op=AluOpType.mult)
            k.tr([(pst(j), Lm[:, j * CH:(j + 1) * CH], c["eye"][:]) for j in range(NP)])
            for j in range(NP):
                k.v('tensor_copy', out=Lt[:, j * CH:(j + 1) * CH], in_=pst(j))
            k.v('scalar_tensor_tensor', out=v3(X), in0=v3(Lm), scalar=-1.0, in1=eyeb,
                op0=AluOpType.mult, op1=AluOpType.add)
            k.v('scalar_tensor_tensor', out=v3(Xt), in0=v3(Lt), scalar=-1.0, in1=eyeb,
                op0=AluOpType.mult, op1=AluOpType.add)
            k.v('tensor_tensor', out=v3(Lt), in0=v3(Lt), in1=eyeb, op=AluOpType.add)
            for it in range(6):
                k.mm([dict(out=pst(j), lhsT=Lt[:, j * CH:(j + 1) * CH],
                           rhs=X[:, j * CH:(j + 1) * CH], start=True, stop=True)
                      for j in range(NP)])
                for j in range(NP):
                    k.v('scalar_tensor_tensor', out=v3(M)[:, j, :], in0=pst(j),
                        scalar=-1.0, in1=twob[:, j, :], op0=AluOpType.mult,
                        op1=AluOpType.add)
                k.mm([dict(out=pst(j), lhsT=Xt[:, j * CH:(j + 1) * CH],
                           rhs=M[:, j * CH:(j + 1) * CH], start=True, stop=True)
                      for j in range(NP)])
                for j in range(NP):
                    k.v('tensor_copy', out=tt2[:, j * CH:(j + 1) * CH], in_=pst(j))
                k.mm([dict(out=pst(j), lhsT=M[:, j * CH:(j + 1) * CH],
                           rhs=Xt[:, j * CH:(j + 1) * CH], start=True, stop=True)
                      for j in range(NP)])
                for j in range(NP):
                    k.v('tensor_copy', out=Xt[:, j * CH:(j + 1) * CH], in_=pst(j))
                k.v('tensor_copy', out=X[:], in_=tt2[:])
            k.v('tensor_copy', out=Sbf[:], in_=S[:])
            k.mm([dict(out=psC[0:128, j * 64:(j + 1) * 64],
                       lhsT=kem[:, j * T + cc * CH:j * T + (cc + 1) * CH],
                       rhs=Sbf[:, j * 64:(j + 1) * 64], start=True, stop=True)
                  for j in range(NP)])
            k.mm([dict(out=psD[0:128, j * 64:(j + 1) * 64],
                       lhsT=qem[:, j * T + cc * CH:j * T + (cc + 1) * CH],
                       rhs=Sbf[:, j * 64:(j + 1) * 64], start=True, stop=True)
                  for j in range(NP)])
            for j in range(NP):
                k.v('tensor_copy', out=Vf[:, j * 64:(j + 1) * 64],
                    in_=vtm[:, (j * NCH + cc) * 64:(j * NCH + cc) * 64 + 64])
            k.v('tensor_tensor', out=v64(RHS),
                in0=psC[0:128, 0:NP * 64].rearrange("p (j f) -> p j f", f=64), in1=gac,
                op=AluOpType.mult)
            k.v('tensor_tensor', out=RHS[:], in0=Vf[:], in1=RHS[:], op=AluOpType.subtract)
            k.v('tensor_tensor', out=v64(RHS), in0=v64(RHS), in1=bec64, op=AluOpType.mult)
            k.mm([dict(out=psC[0:128, j * 64:(j + 1) * 64], lhsT=Xt[:, j * CH:(j + 1) * CH],
                       rhs=RHS[:, j * 64:(j + 1) * 64], start=True, stop=True)
                  for j in range(NP)])
            k.v('tensor_copy', out=Wsb[:], in_=psC[0:128, 0:NP * 64])
            k.v('tensor_tensor', out=AqT[:], in0=ratioT[:], in1=kqsb[:], op=AluOpType.mult)
            k.v('tensor_tensor', out=v3(AqT), in0=v3(AqT), in1=incb, op=AluOpType.mult)
            k.mm([dict(out=psC[0:128, j * 64:(j + 1) * 64], lhsT=AqT[:, j * CH:(j + 1) * CH],
                       rhs=Wsb[:, j * 64:(j + 1) * 64], start=True, stop=True)
                  for j in range(NP)])
            k.v('tensor_tensor', out=v64(Osb),
                in0=psD[0:128, 0:NP * 64].rearrange("p (j f) -> p j f", f=64), in1=gac,
                op=AluOpType.mult)
            k.v('tensor_tensor', out=Osb[:], in0=Osb[:], in1=psC[0:128, 0:NP * 64],
                op=AluOpType.add)
            k.dma(cell_o[HO:HO + NP, cc * CH:(cc + 1) * CH, :].rearrange(
                "j t f -> t j f"), Osb[:].rearrange("p (j f) -> p j f", f=64))
            k.v('tensor_tensor', out=v64(Wsb), in0=v64(Wsb), in1=grc, op=AluOpType.mult)
            k.v('tensor_copy', out=Wbf[:], in_=Wsb[:])
            k.mm([dict(out=psE[0:64, j * 64:(j + 1) * 64],
                       lhsT=ktm[:, (j * NCH + cc) * 64:(j * NCH + cc) * 64 + 64],
                       rhs=Wbf[:, j * 64:(j + 1) * 64], start=True, stop=True)
                  for j in range(NP)])
            gcrow = gc0[0:1, :].rearrange("o (j cc) -> o j cc", cc=NCH)[
                :, :, cc:cc + 1].rearrange("o j cc -> o (j cc)")
            k.mm([dict(out=psD[0:64, 0:NP], lhsT=c["ones1"][0:1, 0:64], rhs=gcrow,
                       start=True, stop=True)])
            k.v('tensor_copy', out=gc[:], in_=psD[0:64, 0:NP])
            k.v('tensor_tensor', out=S[:].rearrange("p (j f) -> p j f", f=64),
                in0=S[:].rearrange("p (j f) -> p j f", f=64),
                in1=gc[:].unsqueeze(2).broadcast_to([64, NP, 64]), op=AluOpType.mult)
            k.v('tensor_tensor', out=S[:], in0=S[:], in1=psE[0:64, 0:NP * 64],
                op=AluOpType.add)
    k.done()
    return k


def build_post0():
    k = K("post0")
    c = consts(k, ("eye", "ones1", "ones128"))
    x_i = k.inp("x", [D, TH])
    cell_i = k.inp("cell", [D, TH])
    ffn_i = k.inp("ffn0", [D, TH], BF16)
    opT = k.inp("opT", [D, D])
    rw1 = k.inp("rw1", [D, 512])
    rw2 = k.inp("rw2", [128, 4])
    rb1c = k.inp("rb1c", [128, 4])
    rb2 = k.inp("rb2", [1, 1])
    lnwc = k.inp("lnwc", [128, 8])
    lnbc = k.inp("lnbc", [128, 8])
    x1_o = k.out("x1", [D, TH])
    xn1f_o = k.out("xn1f", [D, TH])
    xn1b_o = k.out("xn1b", [D, TH], BF16)
    xn1i_o = k.out("xn1i", [D, TH])
    sc_o = k.out("scores", [1, TH])

    psA = k.ps("psA", [128, 512], F32)
    psB = k.ps("psB", [128, 512], F32)
    buf = k.sb("buf", [128, 8 * TH], F32)
    abf = k.sb("abf", [128, 8 * TH], BF16)
    rowb = k.sb("rowb", [2, TH], F32)
    repb = k.sb("repb", [128, 2 * TH], F32)
    stage = k.sb("stage", [128, 1024], F32)
    stgb = k.sb("stgb", [128, 512], BF16)
    stg2 = k.sb("stg2", [128, 512], F32)
    wbf = k.sb("wbf", [128, 8 * 1024], BF16)
    tmp = k.sb("tmp", [128, 512], F32)
    hbf = k.sb("hbf", [128, 4 * TH], BF16)
    scs = k.sb("scs", [1, TH], F32)
    lnws = k.sb("lnws", [128, 8], F32)
    lnbs = k.sb("lnbs", [128, 8], F32)
    k.dma(lnws[:], lnwc)
    k.dma(lnbs[:], lnbc)

    for kc in range(8):
        for hh in range(2):
            k.dma(stage[:], cell_i[kc * 128:(kc + 1) * 128, hh * 1024:(hh + 1) * 1024])
            k.v('tensor_copy', out=abf[:, kc * TH + hh * 1024:kc * TH + (hh + 1) * 1024],
                in_=stage[:])
    load_w(k, opT, wbf, D, D, stage)

    def ev_x1(m0, mw, t0, tw, ps):
        kc = m0 // 128
        k.dma(stage[:, 0:tw], x_i[m0:m0 + 128, t0:t0 + tw])
        k.dma(stgb[:, 0:tw], ffn_i[m0:m0 + 128, t0:t0 + tw])
        k.v('tensor_copy', out=stg2[:, 0:tw], in_=stgb[:, 0:tw])
        k.v('tensor_tensor', out=tmp[:, 0:tw], in0=ps, in1=stage[:, 0:tw],
            op=AluOpType.add)
        k.v('tensor_tensor', out=buf[:, kc * TH + t0:kc * TH + t0 + tw],
            in0=tmp[:, 0:tw], in1=stg2[:, 0:tw], op=AluOpType.add)
        k.dma(x1_o[m0:m0 + 128, t0:t0 + tw], buf[:, kc * TH + t0:kc * TH + t0 + tw])
    mm_fm(k, lambda kc, m0, mw: wbf[:, kc * D + m0:kc * D + m0 + mw],
          lambda kc, t0, tw: abf[:, kc * TH + t0:kc * TH + t0 + tw],
          D, D, TH, ev_x1, psA, psB)

    def cb_xn1(kc, sl):
        k.v('tensor_copy', out=abf[:, kc * TH:(kc + 1) * TH], in_=sl)
        k.dma(xn1b_o[kc * 128:(kc + 1) * 128, :], abf[:, kc * TH:(kc + 1) * TH])
        k.dma(xn1f_o[kc * 128:(kc + 1) * 128, :], sl)
    ln_fm(k, buf, TH, c, psA, psB, rowb, repb, out_cb=cb_xn1)

    def cb_xn1i(kc, sl):
        k.v('tensor_copy', out=abf[:, kc * TH:(kc + 1) * TH], in_=sl)
        k.dma(xn1i_o[kc * 128:(kc + 1) * 128, :], sl)
    ln_fm(k, buf, TH, c, psA, psB, rowb, repb,
          affine=(lambda kc: lnws[:, kc:kc + 1], lambda kc: lnbs[:, kc:kc + 1]),
          out_cb=cb_xn1i)
    load_w(k, rw1, wbf, D, 512, stage)

    def ev_h(m0, mw, t0, tw, ps):
        mc = m0 // 128
        k.s('activation', out=tmp[:, 0:tw], in_=ps, func=ACT.Gelu_apprx_tanh,
            bias=rb1s[:, mc:mc + 1])
        k.v('tensor_copy', out=hbf[:, mc * TH + t0:mc * TH + t0 + tw], in_=tmp[:, 0:tw])
    rb1s = k.sb("rb1s", [128, 4], F32)
    k.dma(rb1s[:], rb1c)
    mm_fm(k, lambda kc, m0, mw: wbf[:, kc * 512 + m0:kc * 512 + m0 + mw],
          lambda kc, t0, tw: abf[:, kc * TH + t0:kc * TH + t0 + tw],
          D, 512, TH, ev_h, psA, psB)
    w2s = k.sb("w2s", [128, 4], F32)
    w2b = k.sb("w2b", [128, 4], BF16)
    k.dma(w2s[:], rw2)
    k.v('tensor_copy', out=w2b[:], in_=w2s[:])
    rb2s = k.sb("rb2s", [1, 1], F32)
    k.dma(rb2s[:], rb2)
    for t0 in range(0, TH, 512):
        k.mm([dict(out=psA[0:1, 0:512], lhsT=w2b[:, kc:kc + 1],
                   rhs=hbf[:, kc * TH + t0:kc * TH + t0 + 512],
                   start=(kc == 0), stop=(kc == 3)) for kc in range(4)])
        k.v('tensor_scalar', out=scs[0:1, t0:t0 + 512], in0=psA[0:1, 0:512],
            scalar1=rb2s[0:1, 0:1], scalar2=None, op0=AluOpType.add)
    k.dma(sc_o, scs[:])
    k.done()
    return k



def run(name, build_fn, in_maps):
    if name not in _PROG:
        _PROG[name] = build_fn()
    k = _PROG[name]
    cm = const_maps()
    for m in in_maps:
        for nm, arr in cm.items():
            m.setdefault(nm, arr)
    res = run_bass_kernel_spmd(k.nc, in_maps, core_ids=list(range(NC)))
    return [{nm: np.asarray(r[nm]) for nm in k.outs} for r in res.results]



def build_misc1():
    k = K("misc1")
    c = consts(k, ("eye", "ones1"))
    sc_i = k.inp("scores", [4, T])
    pup_i = k.inp("pup", [128, 4096])
    pcn_i = k.inp("pcn", [128, 4096])
    uw_i = k.inp("uw", [12, 4, 4096])
    dw_i = k.inp("dw", [12, 4, 4096])
    brow_i = k.inp("brow", [1, 4096])
    ubias_i = k.inp("ubias", [1, 4096])
    mask_o = k.out("mask", [4, T])
    wup_o = k.out("wup", [128, 4096])
    wcn_o = k.out("wcn", [128, 4096])
    btup_o = k.out("btup", [1, 4096])

    psA = k.ps("psA", [128, 512], F32)
    sc = k.sb("sc", [4, T], F32)
    lo = k.sb("lo", [4, 1], F32)
    hi = k.sb("hi", [4, 1], F32)
    mid = k.sb("mid", [4, 1], F32)
    cm = k.sb("cm", [4, T], F32)
    cnt = k.sb("cnt", [4, 1], F32)
    ge = k.sb("ge", [4, 1], F32)
    k.dma(sc[:], sc_i)
    k.v('memset', lo[:], -10000.0)
    k.v('memset', hi[:], 10000.0)
    for _ in range(60):
        k.v('tensor_tensor', out=mid[:], in0=lo[:], in1=hi[:], op=AluOpType.add)
        k.v('tensor_scalar', out=mid[:], in0=mid[:], scalar1=0.5, scalar2=None,
            op0=AluOpType.mult)
        k.v('tensor_scalar', out=cm[:], in0=sc[:], scalar1=mid[:, 0:1], scalar2=None,
            op0=AluOpType.is_gt)
        k.v('tensor_reduce', out=cnt[:], in_=cm[:], axis=AX.X, op=AluOpType.add)
        k.v('tensor_scalar', out=ge[:], in0=cnt[:], scalar1=1024.0, scalar2=None,
            op0=AluOpType.is_ge)
        k.v('tensor_tensor', out=cnt[:], in0=mid[:], in1=lo[:], op=AluOpType.subtract)
        k.v('tensor_tensor', out=cnt[:], in0=cnt[:], in1=ge[:], op=AluOpType.mult)
        k.v('tensor_tensor', out=lo[:], in0=lo[:], in1=cnt[:], op=AluOpType.add)
        k.v('tensor_scalar', out=ge[:], in0=ge[:], scalar1=-1.0, scalar2=1.0,
            op0=AluOpType.mult, op1=AluOpType.add)
        k.v('tensor_tensor', out=cnt[:], in0=mid[:], in1=hi[:], op=AluOpType.subtract)
        k.v('tensor_tensor', out=cnt[:], in0=cnt[:], in1=ge[:], op=AluOpType.mult)
        k.v('tensor_tensor', out=hi[:], in0=hi[:], in1=cnt[:], op=AluOpType.add)
    k.v('tensor_scalar', out=cm[:], in0=sc[:], scalar1=lo[:, 0:1], scalar2=None,
        op0=AluOpType.is_gt)
    k.dma(mask_o, cm[:])

    pan = k.sb("pan", [128, 4096], F32)
    pan2 = k.sb("pan2", [128, 4096], F32)
    rep = k.sb("rep", [128, 2 * 4096], F32)
    t1 = k.sb("t1", [128, 2048], F32)
    brow = k.sb("brow_s", [1, 4096], F32)
    brow2 = k.sb("brow2_s", [1, 4096], F32)

    vrow = k.sb("vrow", [1, 4 * 2048], F32)

    def fold(win, src_sb, dst_sb, order, do_bias):
        cur, oth = src_sb, dst_sb
        bcur, both = brow, brow2
        for s in order:
            st = 2 ** s
            nb = 4096 // (2 * st)
            for i in range(4):
                k.dma(vrow[0:1, i * 2048:(i + 1) * 2048],
                      win[s:s + 1, i, 0:2048].rearrange("a f -> a f"))
            w00A = vrow[0:1, 0:2048]
            w01A = vrow[0:1, 2048:4096]
            w00B = vrow[0:1, 4096:6144]
            w01B = vrow[0:1, 6144:8192]
            for i, row in enumerate((w00A, w01A, w00B, w01B)):
                dst = rep[:, i * 2048:(i + 1) * 2048]
                for t0 in range(0, 2048, 512):
                    k.mm([dict(out=psA[0:128, 0:512], lhsT=c["ones1"][:],
                               rhs=row[:, t0:t0 + 512], start=True, stop=True)])
                    k.v('tensor_copy', out=dst[:, t0:t0 + 512], in_=psA[0:128, 0:512])

            def vw(tsb, h):
                return tsb[:].rearrange("p (nb two l) -> p nb two l", two=2, l=st)[
                    :, :, h:h + 1, :]

            def vr(i):
                return rep[:, i * 2048:(i + 1) * 2048].rearrange(
                    "p (nb l) -> p nb l", l=st).unsqueeze(2)
            k.v('tensor_tensor', out=vw(oth, 0), in0=vw(cur, 0), in1=vr(0),
                op=AluOpType.mult)
            k.v('tensor_tensor', out=t1[:].rearrange("p (nb l) -> p nb l",
                                                     l=st).unsqueeze(2),
                in0=vw(cur, 1), in1=vr(1), op=AluOpType.mult)
            k.v('tensor_tensor', out=vw(oth, 0), in0=vw(oth, 0),
                in1=t1[:].rearrange("p (nb l) -> p nb l", l=st).unsqueeze(2),
                op=AluOpType.add)
            k.v('tensor_tensor', out=vw(oth, 1), in0=vw(cur, 1), in1=vr(2),
                op=AluOpType.mult)
            k.v('tensor_tensor', out=t1[:].rearrange("p (nb l) -> p nb l",
                                                     l=st).unsqueeze(2),
                in0=vw(cur, 0), in1=vr(3), op=AluOpType.mult)
            k.v('tensor_tensor', out=vw(oth, 1), in0=vw(oth, 1),
                in1=t1[:].rearrange("p (nb l) -> p nb l", l=st).unsqueeze(2),
                op=AluOpType.add)
            if do_bias:
                def bw(tsb, h):
                    return tsb.rearrange("o (nb two l) -> o nb two l", two=2, l=st)[
                        :, :, h:h + 1, :]

                def br(row):
                    return row.rearrange("o (nb l) -> o nb l", l=st).unsqueeze(2)
                k.v('tensor_tensor', out=bw(both[0:1, :], 0), in0=bw(bcur[0:1, :], 0),
                    in1=br(w00A), op=AluOpType.mult)
                k.v('tensor_tensor', out=bw(both[0:1, :], 1), in0=bw(bcur[0:1, :], 1),
                    in1=br(w00B), op=AluOpType.mult)
                k.v('tensor_tensor', out=t1[0:1, 0:2048].rearrange(
                    "o (nb l) -> o nb l", l=st).unsqueeze(2), in0=bw(bcur[0:1, :], 1),
                    in1=br(w01A), op=AluOpType.mult)
                k.v('tensor_tensor', out=bw(both[0:1, :], 0), in0=bw(both[0:1, :], 0),
                    in1=t1[0:1, 0:2048].rearrange("o (nb l) -> o nb l",
                                                  l=st).unsqueeze(2), op=AluOpType.add)
                k.v('tensor_tensor', out=t1[0:1, 0:2048].rearrange(
                    "o (nb l) -> o nb l", l=st).unsqueeze(2), in0=bw(bcur[0:1, :], 0),
                    in1=br(w01B), op=AluOpType.mult)
                k.v('tensor_tensor', out=bw(both[0:1, :], 1), in0=bw(both[0:1, :], 1),
                    in1=t1[0:1, 0:2048].rearrange("o (nb l) -> o nb l",
                                                  l=st).unsqueeze(2), op=AluOpType.add)
                bcur, both = both, bcur
            cur, oth = oth, cur
        return cur, bcur

    k.dma(pan[:], pup_i)
    k.dma(brow[0:1, :], brow_i)
    res, bres = fold(uw_i, pan, pan2, list(range(12)), True)
    k.dma(wup_o, res[:])
    ub = k.sb("ub_s", [1, 4096], F32)
    k.dma(ub[0:1, :], ubias_i)
    k.v('tensor_tensor', out=bres[0:1, :], in0=bres[0:1, :], in1=ub[0:1, :],
        op=AluOpType.add)
    k.dma(btup_o, bres[0:1, :])
    k.dma(pan[:], pcn_i)
    res2, _ = fold(dw_i, pan, pan2, list(range(11, -1, -1)), False)
    k.dma(wcn_o, res2[:])
    k.done()
    return k



def build_attn1():
    k = K("attn1")
    c = consts(k, ("eye", "ones1", "ones128b"))
    x_i = k.inp("xat", [D, 1024])
    wq_i = k.inp("wq", [D, 512])
    wk_i = k.inp("wk", [D, 512])
    wv_i = k.inp("wv", [D, 512])
    bq_i = k.inp("bq", [64, 8])
    bk_i = k.inp("bk", [64, 8])
    bv_i = k.inp("bv", [1, 512])
    o_o = k.out("o", [8, 1024, 64])
    LA = 1024

    psA = k.ps("psA", [128, 512], F32)
    psB = k.ps("psB", [128, 512], F32)
    psZ = k.ps("psZ", [128, 64], F32)
    xb = k.sb("xb", [128, 8 * LA], BF16)
    stage = k.sb("stage", [128, 1024], F32)
    wqb = k.sb("wqb", [128, 8 * 512], BF16)
    wkb = k.sb("wkb", [128, 8 * 512], BF16)
    wvb = k.sb("wvb", [128, 8 * 512], BF16)
    bq = k.sb("bq_s", [64, 8], F32)
    bk = k.sb("bk_s", [64, 8], F32)
    bv = k.sb("bv_s", [1, 512], F32)
    bvr = k.sb("bvr", [128, 64], F32)
    k.dma(bq[:], bq_i)
    k.dma(bk[:], bk_i)
    k.dma(bv[:], bv_i)
    for kc in range(8):
        k.dma(stage[:], x_i[kc * 128:(kc + 1) * 128, :])
        k.v('tensor_copy', out=xb[:, kc * LA:(kc + 1) * LA], in_=stage[:])
    load_w(k, wq_i, wqb, D, 512, stage)
    load_w(k, wk_i, wkb, D, 512, stage)
    load_w(k, wv_i, wvb, D, 512, stage)
    qe = k.sb("qe", [64, LA], BF16)
    ke = k.sb("ke", [64, LA], BF16)
    vt = k.sb("vt", [128, 8 * 64], BF16)
    sT = k.sb("sT", [128, 8 * LA], F32)
    pbf = k.sb("pbf", [128, 8 * LA], BF16)
    mx = k.sb("mx", [128, 16], F32)
    mrow = k.sb("mrow", [1, 128], F32)
    mneg = k.sb("mneg", [128, 1], F32)
    zz = k.sb("zz", [128, 8], F32)
    ot = k.sb("ot", [128, 64], F32)
    tmpf = k.sb("tmpf", [128, 512], F32)
    for h in range(8):
        for dst, wsb, bb in ((qe, wqb, bq), (ke, wkb, bk)):
            for t0 in range(0, LA, 512):
                k.mm([dict(out=psA[0:64, 0:512],
                           lhsT=wsb[:, kc * 512 + h * 64:kc * 512 + (h + 1) * 64],
                           rhs=xb[:, kc * LA + t0:kc * LA + t0 + 512],
                           start=(kc == 0), stop=(kc == 7)) for kc in range(8)])
                k.v('tensor_scalar', out=tmpf[0:64, 0:512], in0=psA[0:64, 0:512],
                    scalar1=bb[:, h:h + 1], scalar2=None, op0=AluOpType.add)
                k.v('tensor_copy', out=dst[:, t0:t0 + 512], in_=tmpf[0:64, 0:512])
        k.mm([dict(out=psB[0:128, 0:64], lhsT=c["ones1"][:],
                   rhs=bv[0:1, h * 64:(h + 1) * 64], start=True, stop=True)])
        k.v('tensor_copy', out=bvr[:], in_=psB[0:128, 0:64])
        for tt in range(8):
            k.mm([dict(out=psA[0:128, 0:64],
                       lhsT=xb[:, kc * LA + tt * 128:kc * LA + (tt + 1) * 128],
                       rhs=wvb[:, kc * 512 + h * 64:kc * 512 + (h + 1) * 64],
                       start=(kc == 0), stop=(kc == 7)) for kc in range(8)])
            k.v('tensor_tensor', out=tmpf[:, 0:64], in0=psA[0:128, 0:64],
                in1=bvr[:], op=AluOpType.add)
            k.v('tensor_copy', out=vt[:, tt * 64:(tt + 1) * 64], in_=tmpf[:, 0:64])
        for jt in range(8):
            for ih in range(2):
                k.mm([dict(out=psA[0:128, 0:512],
                           lhsT=ke[:, jt * 128:(jt + 1) * 128],
                           rhs=qe[:, ih * 512:(ih + 1) * 512],
                           start=True, stop=True)])
                k.v('tensor_copy',
                    out=sT[:, jt * LA + ih * 512:jt * LA + ih * 512 + 512],
                    in_=psA[0:128, 0:512])
            k.v('tensor_reduce', out=mx[:, jt:jt + 1],
                in_=sT[:, jt * LA:(jt + 1) * LA], axis=AX.X, op=AluOpType.max)
        k.v('tensor_reduce', out=mx[:, 8:9], in_=mx[:, 0:8], axis=AX.X,
            op=AluOpType.max)
        k.tr([(psB[0:1, 0:128], mx[:, 8:9], c["eye"][:])])
        k.v('tensor_copy', out=mrow[0:1, :], in_=psB[0:1, 0:128])
        k.v('tensor_reduce', out=mrow[0:1, 0:1], in_=mrow[0:1, :], axis=AX.X,
            op=AluOpType.max)
        k.v('tensor_scalar', out=mrow[0:1, 0:1], in0=mrow[0:1, 0:1], scalar1=-0.125,
            scalar2=None, op0=AluOpType.mult)
        k.mm([dict(out=psB[0:128, 0:1], lhsT=c["ones1"][:], rhs=mrow[0:1, 0:1],
                   start=True, stop=True)])
        k.v('tensor_copy', out=mneg[:], in_=psB[0:128, 0:1])
        for jt in range(8):
            k.v('tensor_scalar', out=sT[:, jt * LA:(jt + 1) * LA],
                in0=sT[:, jt * LA:(jt + 1) * LA], scalar1=0.125, scalar2=mneg[:, 0:1],
                op0=AluOpType.mult, op1=AluOpType.add)
            k.s('activation', out=pbf[:, jt * LA:(jt + 1) * LA],
                in_=sT[:, jt * LA:(jt + 1) * LA], func=ACT.Exp)
        for it in range(8):
            k.mm([dict(out=psZ[0:128, 0:1],
                       lhsT=pbf[:, jt * LA + it * 128:jt * LA + (it + 1) * 128],
                       rhs=c["ones128b"][:], start=(jt == 0), stop=(jt == 7))
                  for jt in range(8)])
            k.v('tensor_copy', out=zz[:, it:it + 1], in_=psZ[0:128, 0:1])
        k.v('reciprocal', out=zz[:], in_=zz[:])
        for it in range(8):
            k.mm([dict(out=psZ[0:128, 0:64],
                       lhsT=pbf[:, jt * LA + it * 128:jt * LA + (it + 1) * 128],
                       rhs=vt[:, jt * 64:(jt + 1) * 64], start=(jt == 0),
                       stop=(jt == 7)) for jt in range(8)])
            k.v('tensor_scalar', out=ot[:], in0=psZ[0:128, 0:64],
                scalar1=zz[:, it:it + 1], scalar2=None, op0=AluOpType.mult)
            k.dma(o_o[h, it * 128:(it + 1) * 128, :], ot[:])
    k.done()
    return k


def build_gemm_small():
    k = K("gemm_small")
    c = consts(k, ("ones1",))
    o_i = k.inp("ofm", [D, 512])
    wo_i = k.inp("wo", [D, D])
    boc_i = k.inp("boc", [128, 8])
    d_o = k.out("dattn", [D, 512])
    psA = k.ps("psA", [128, 512], F32)
    psB = k.ps("psB", [128, 512], F32)
    ob = k.sb("ob", [128, 8 * 512], BF16)
    stage = k.sb("stage", [128, 1024], F32)
    wbf = k.sb("wbf", [128, 8 * D], BF16)
    boc = k.sb("boc_s", [128, 8], F32)
    tmpf = k.sb("tmpf", [128, 512], F32)
    k.dma(boc[:], boc_i)
    for kc in range(8):
        k.dma(stage[:, 0:512], o_i[kc * 128:(kc + 1) * 128, :])
        k.v('tensor_copy', out=ob[:, kc * 512:(kc + 1) * 512], in_=stage[:, 0:512])
    load_w(k, wo_i, wbf, D, D, stage)

    def ev(m0, mw, t0, tw, ps):
        k.v('tensor_scalar', out=tmpf[0:mw, 0:tw], in0=ps,
            scalar1=boc[:, m0 // 128:m0 // 128 + 1], scalar2=None, op0=AluOpType.add)
        k.dma(d_o[m0:m0 + mw, t0:t0 + tw], tmpf[0:mw, 0:tw])
    mm_fm(k, lambda kc, m0, mw: wbf[:, kc * D + m0:kc * D + m0 + mw],
          lambda kc, t0, tw: ob[:, kc * 512 + t0:kc * 512 + t0 + tw],
          D, D, 512, ev, psA, psB)
    k.done()
    return k



def build_ssm_in():
    k = K("ssm_in")
    c = consts(k, ("ones1",))
    TS = 1536
    x_i = k.inp("xs", [D, TS])
    w_i = k.inp("inw", [D, 4096])
    xi_o = k.out("xi", [2048, TS])
    z_o = k.out("z", [2048, TS])
    psA = k.ps("psA", [128, 512], F32)
    psB = k.ps("psB", [128, 512], F32)
    xb = k.sb("xb", [128, 8 * TS], BF16)
    stage = k.sb("stage", [128, 4096], F32)
    wbf = k.sb("wbf", [128, 8 * 4096], BF16)
    tmpf = k.sb("tmpf", [128, 512], F32)
    for kc in range(8):
        k.dma(stage[:, 0:TS], x_i[kc * 128:(kc + 1) * 128, :])
        k.v('tensor_copy', out=xb[:, kc * TS:(kc + 1) * TS], in_=stage[:, 0:TS])
    load_w(k, w_i, wbf, D, 4096, stage)

    def ev(m0, mw, t0, tw, ps):
        k.v('tensor_copy', out=tmpf[0:mw, 0:tw], in_=ps)
        if m0 < 2048:
            k.dma(xi_o[m0:m0 + mw, t0:t0 + tw], tmpf[0:mw, 0:tw])
        else:
            k.dma(z_o[m0 - 2048:m0 - 2048 + mw, t0:t0 + tw], tmpf[0:mw, 0:tw])
    mm_fm(k, lambda kc, m0, mw: wbf[:, kc * 4096 + m0:kc * 4096 + m0 + mw],
          lambda kc, t0, tw: xb[:, kc * TS + t0:kc * TS + t0 + tw],
          D, 4096, TS, ev, psA, psB)
    k.done()
    return k


def build_ssm_conv():
    k = K("ssm_conv")
    c = consts(k, ("ones1",))
    LS = 3072
    xi_i = k.inp("xisl", [256, 4 * LS])
    cw_i = k.inp("cwsl", [128, 2 * 4])
    cb_i = k.inp("cbsl", [128, 2])
    xc_o = k.out("xc", [256, 4 * LS])
    xi = k.sb("xi_s", [128, 2 * 4 * LS], F32)
    cws = k.sb("cws", [128, 8], F32)
    cbs = k.sb("cbs", [128, 2], F32)
    acc = k.sb("acc", [128, LS], F32)
    tmp = k.sb("tmp", [128, LS], F32)
    k.dma(cws[:], cw_i)
    k.dma(cbs[:], cb_i)
    for kc in range(2):
        k.dma(xi[:, kc * 4 * LS:(kc + 1) * 4 * LS], xi_i[kc * 128:(kc + 1) * 128, :])
    for kc in range(2):
        for b in range(4):
            sl = xi[:, kc * 4 * LS + b * LS: kc * 4 * LS + (b + 1) * LS]
            k.v('tensor_scalar', out=acc[:], in0=sl, scalar1=cws[:, kc * 4 + 3:kc * 4 + 4],
                scalar2=None, op0=AluOpType.mult)
            for j in range(3):
                sh = 3 - j
                k.v('tensor_scalar', out=tmp[:, sh:LS], in0=sl[:, 0:LS - sh],
                    scalar1=cws[:, kc * 4 + j:kc * 4 + j + 1], scalar2=None,
                    op0=AluOpType.mult)
                k.v('memset', tmp[:, 0:sh], 0.0)
                k.v('tensor_tensor', out=acc[:], in0=acc[:], in1=tmp[:],
                    op=AluOpType.add)
            k.v('tensor_scalar', out=acc[:], in0=acc[:], scalar1=cbs[:, kc:kc + 1],
                scalar2=None, op0=AluOpType.add)
            silu(k, acc[:], acc[:], tmp[:])
            k.dma(xc_o[kc * 128:(kc + 1) * 128, b * LS:(b + 1) * LS], acc[:])
    k.done()
    return k


def build_ssm_proj():
    k = K("ssm_proj")
    c = consts(k, ("ones1",))
    TS = 1536
    xc_i = k.inp("xcs", [2048, TS])
    xp_i = k.inp("xprojw", [2048, 96])
    dtw_i = k.inp("dtw", [64, 2048])
    dtb_i = k.inp("dtbc", [128, 16])
    de_o = k.out("delta", [2048, TS])
    dx_o = k.out("dx", [2048, TS])
    bm_o = k.out("bm", [16, TS])
    cm_o = k.out("cm", [16, TS])
    psA = k.ps("psA", [128, 512], F32)
    psB = k.ps("psB", [128, 512], F32)
    xc = k.sb("xc_s", [128, 16 * TS], F32)
    xcb = k.sb("xcb", [128, 16 * TS], BF16)
    stage = k.sb("stage", [128, 2048], F32)
    wpb = k.sb("wpb", [128, 16 * 96], BF16)
    dtwb = k.sb("dtwb", [64, 2048], BF16)
    dtb = k.sb("dtb_s", [128, 16], F32)
    prj = k.sb("prj", [128, TS], F32)
    dtp = k.sb("dtp", [64, TS], BF16)
    tmp = k.sb("tmp", [128, TS], F32)
    spt = k.sb("spt", [128, TS], F32)
    k.dma(dtb[:], dtb_i)
    for kc in range(16):
        k.dma(xc[:, kc * TS:(kc + 1) * TS], xc_i[kc * 128:(kc + 1) * 128, :])
        k.v('tensor_copy', out=xcb[:, kc * TS:(kc + 1) * TS],
            in_=xc[:, kc * TS:(kc + 1) * TS])
        k.dma(stage[:, 0:96], xp_i[kc * 128:(kc + 1) * 128, :])
        k.v('tensor_copy', out=wpb[:, kc * 96:(kc + 1) * 96], in_=stage[:, 0:96])
    k.dma(stage[0:64, 0:2048], dtw_i)
    k.v('tensor_copy', out=dtwb[:], in_=stage[0:64, 0:2048])
    for t0 in range(0, TS, 512):
        k.mm([dict(out=psA[0:96, 0:512], lhsT=wpb[:, kc * 96:(kc + 1) * 96],
                   rhs=xcb[:, kc * TS + t0:kc * TS + t0 + 512],
                   start=(kc == 0), stop=(kc == 15)) for kc in range(16)])
        k.v('tensor_copy', out=prj[0:96, t0:t0 + 512], in_=psA[0:96, 0:512])
    k.v('tensor_copy', out=dtp[0:64, :], in_=prj[0:64, :])
    k.dma(bm_o, prj[64:80, :])
    k.dma(cm_o, prj[80:96, :])
    for mc in range(16):
        for t0 in range(0, TS, 512):
            k.mm([dict(out=psA[0:128, 0:512],
                       lhsT=dtwb[:, mc * 128:(mc + 1) * 128],
                       rhs=dtp[0:64, t0:t0 + 512], start=True, stop=True)])
            k.v('tensor_scalar', out=tmp[:, t0:t0 + 512], in0=psA[0:128, 0:512],
                scalar1=dtb[:, mc:mc + 1], scalar2=None, op0=AluOpType.add)
        softplus(k, tmp[:], tmp[:], spt[:])
        k.dma(de_o[mc * 128:(mc + 1) * 128, :], tmp[:])
        k.v('tensor_tensor', out=tmp[:], in0=tmp[:], in1=xc[:, mc * TS:(mc + 1) * TS],
            op=AluOpType.mult)
        k.dma(dx_o[mc * 128:(mc + 1) * 128, :], tmp[:])
    k.done()
    return k


def build_ssm_scan():
    k = K("ssm_scan")
    c = consts(k, ("p8", "p16", "q8"))
    LS = 3072
    de_i = k.inp("des", [256, 4 * LS])
    dx_i = k.inp("dxs", [256, 4 * LS])
    bm_i = k.inp("bms", [16, 4 * LS])
    cm_i = k.inp("cms", [16, 4 * LS])
    al_i = k.inp("alane", [128, 32])
    y_o = k.out("y", [256, 4 * LS])
    psA = k.ps("psA", [128, 512], F32)
    psy = k.ps("psy", [8, 512], F32)
    bm = k.sb("bm_s", [16, LS], F32)
    cm = k.sb("cm_s", [16, LS], F32)
    bmr = k.sb("bmr", [128, LS], F32)
    cmr = k.sb("cmr", [128, LS], F32)
    alan = k.sb("alan", [128, 32], F32)
    des = k.sb("des_s", [128, LS], F32)
    dxs = k.sb("dxs_s", [128, LS], F32)
    der = k.sb("der", [128, LS], F32)
    dxr = k.sb("dxr", [128, LS], F32)
    d0 = k.sb("d0", [128, LS], F32)
    hh = k.sb("hh", [128, LS], F32)
    yb = k.sb("yb", [8, LS], F32)
    k.dma(alan[:], al_i)
    k.s('activation', out=alan[:], in_=alan[:], func=ACT.Exp)
    k.v('tensor_scalar', out=alan[:], in0=alan[:], scalar1=-1.0, scalar2=None,
        op0=AluOpType.mult)
    for b in range(4):
        k.dma(bm[:], bm_i[:, b * LS:(b + 1) * LS])
        k.dma(cm[:], cm_i[:, b * LS:(b + 1) * LS])
        for dst, srow in ((bmr, bm), (cmr, cm)):
            for t0 in range(0, LS, 512):
                k.mm([dict(out=psA[0:128, 0:512], lhsT=c["p16"][:],
                           rhs=srow[0:16, t0:t0 + 512], start=True, stop=True)])
                k.v('tensor_copy', out=dst[:, t0:t0 + 512], in_=psA[0:128, 0:512])
        for db in range(32):
            k.dma(des[0:8, :], de_i[db * 8:(db + 1) * 8, b * LS:(b + 1) * LS])
            k.dma(dxs[0:8, :], dx_i[db * 8:(db + 1) * 8, b * LS:(b + 1) * LS])
            for src_, dst in ((des, der), (dxs, dxr)):
                for t0 in range(0, LS, 512):
                    k.mm([dict(out=psA[0:128, 0:512], lhsT=c["p8"][:],
                               rhs=src_[0:8, t0:t0 + 512], start=True, stop=True)])
                    k.v('tensor_copy', out=dst[:, t0:t0 + 512], in_=psA[0:128, 0:512])
            k.s('activation', out=d0[:], in_=der[:], func=ACT.Exp,
                scale=alan[:, db:db + 1])
            k.v('tensor_tensor', out=dxr[:], in0=dxr[:], in1=bmr[:], op=AluOpType.mult)
            scan_seg(k, hh, d0, dxr, LS)
            k.v('tensor_tensor', out=hh[:], in0=hh[:], in1=cmr[:], op=AluOpType.mult)
            for t0 in range(0, LS, 512):
                k.mm([dict(out=psy[0:8, 0:512], lhsT=c["q8"][:],
                           rhs=hh[:, t0:t0 + 512], start=True, stop=True)])
                k.v('tensor_copy', out=yb[0:8, t0:t0 + 512], in_=psy[0:8, 0:512])
            k.dma(y_o[db * 8:(db + 1) * 8, b * LS:(b + 1) * LS], yb[0:8, :])
    k.done()
    return k


def build_ssm_gate():
    k = K("ssm_gate")
    c = consts(k, ("ones1",))
    TS = 1536
    y_i = k.inp("ys", [2048, TS])
    xc_i = k.inp("xcs2", [2048, TS])
    z_i = k.inp("zs", [2048, TS])
    ow_i = k.inp("outw", [2048, D])
    dpc_i = k.inp("dpc", [128, 16])
    d_o = k.out("dssm", [D, TS])
    psA = k.ps("psA", [128, 512], F32)
    ytk = k.sb("ytk", [128, 16 * TS], BF16)
    stage = k.sb("stage", [128, TS], F32)
    st2 = k.sb("st2", [128, TS], F32)
    st3 = k.sb("st3", [128, TS], F32)
    wbf = k.sb("wbf", [128, 16 * D], BF16)
    dpc = k.sb("dpc_s", [128, 16], F32)
    acc = k.sb("acc", [128, 8 * 512], F32)
    k.dma(dpc[:], dpc_i)
    for kc in range(16):
        k.dma(st2[:], y_i[kc * 128:(kc + 1) * 128, :])
        k.dma(st3[:], xc_i[kc * 128:(kc + 1) * 128, :])
        k.v('tensor_scalar', out=st3[:], in0=st3[:], scalar1=dpc[:, kc:kc + 1],
            scalar2=None, op0=AluOpType.mult)
        k.v('tensor_tensor', out=st2[:], in0=st2[:], in1=st3[:], op=AluOpType.add)
        k.dma(st3[:], z_i[kc * 128:(kc + 1) * 128, :])
        silu(k, st3[:], st3[:], stage[:, 0:TS])
        k.v('tensor_tensor', out=st2[:], in0=st2[:], in1=st3[:], op=AluOpType.mult)
        k.v('tensor_copy', out=ytk[:, kc * TS:(kc + 1) * TS], in_=st2[:])
        k.dma(stage[:, 0:D], ow_i[kc * 128:(kc + 1) * 128, :])
        k.v('tensor_copy', out=wbf[:, kc * D:(kc + 1) * D], in_=stage[:, 0:D])
    for t0 in range(0, TS, 512):
        for fb in range(8):
            k.mm([dict(out=psA[0:128, 0:512],
                       lhsT=wbf[:, kc * D + fb * 128:kc * D + (fb + 1) * 128],
                       rhs=ytk[:, kc * TS + t0:kc * TS + t0 + 512],
                       start=(kc == 0), stop=(kc == 15)) for kc in range(16)])
            k.v('tensor_copy', out=acc[:, fb * 512:(fb + 1) * 512],
                in_=psA[0:128, 0:512])
        for fb in range(8):
            k.dma(d_o[fb * 128:(fb + 1) * 128, t0:t0 + 512],
                  acc[:, fb * 512:(fb + 1) * 512])
    k.done()
    return k



def build_ffn1up():
    k = K("ffn1up")
    c = consts(k, ("ones1",))
    x_i = k.inp("xn1b", [D, TH], BF16)
    w_i = k.inp("wupf", [D, 4096])
    bt_i = k.inp("btupc", [128, 32])
    h_o = k.out("h1", [4096, TH], BF16)
    psA = k.ps("psA", [128, 512], F32)
    psB = k.ps("psB", [128, 512], F32)
    xb = k.sb("xb", [128, 8 * TH], BF16)
    stage = k.sb("stage", [128, 4096], F32)
    wbf = k.sb("wbf", [128, 8 * 4096], BF16)
    btc = k.sb("btc", [128, 32], F32)
    tmpb = k.sb("tmpb", [128, 512], BF16)
    tmpf = k.sb("tmpf", [128, 512], F32)
    k.dma(btc[:], bt_i)
    for kc in range(8):
        k.dma(xb[:, kc * TH:(kc + 1) * TH], x_i[kc * 128:(kc + 1) * 128, :])
    load_w(k, w_i, wbf, D, 4096, stage)

    def ev(m0, mw, t0, tw, ps):
        mc = m0 // 128
        k.v('tensor_scalar', out=tmpf[:, 0:tw], in0=ps, scalar1=btc[:, mc:mc + 1],
            scalar2=None, op0=AluOpType.add)
        k.s('activation', out=tmpf[:, 0:tw], in_=tmpf[:, 0:tw], func=ACT.Gelu)
        k.v('tensor_copy', out=tmpb[:, 0:tw], in_=tmpf[:, 0:tw])
        k.dma(h_o[m0:m0 + mw, t0:t0 + tw], tmpb[0:mw, 0:tw])
    mm_fm(k, lambda kc, m0, mw: wbf[:, kc * 4096 + m0:kc * 4096 + m0 + mw],
          lambda kc, t0, tw: xb[:, kc * TH + t0:kc * TH + t0 + tw],
          D, 4096, TH, ev, psA, psB)
    k.done()
    return k


def build_ffn1dn():
    k = K("ffn1dn")
    c = consts(k, ("eye", "ones1", "ones128"))
    h_i = k.inp("h1", [4096, TH], BF16)
    w_i = k.inp("wcnT", [4096, D])
    x1_i = k.inp("x1", [D, TH])
    xn1_i = k.inp("xn1f", [D, TH])
    da_i = k.inp("dattnf", [D, TH])
    ds_i = k.inp("dssmf", [D, TH])
    bc1_i = k.inp("bconc", [128, 8])
    bc2_i = k.inp("dbiasc", [128, 8])
    x2_o = k.out("x2", [D, TH])
    xn2_o = k.out("xn2", [D, TH])
    psA = k.ps("psA", [128, 512], F32)
    psB = k.ps("psB", [128, 512], F32)
    buf = k.sb("buf", [128, 8 * TH], F32)
    repb = k.sb("repb", [128, 2 * TH], F32)
    stage = k.sb("stage", [128, 1024], F32)
    wgrp = k.sb("wgrp", [128, 8 * D], BF16)
    hgrp = k.sb("hgrp", [128, 8 * 512], BF16)
    yac = k.sb("yac", [128, 8 * 512], F32)
    bc1 = k.sb("bc1", [128, 8], F32)
    bc2 = k.sb("bc2", [128, 8], F32)
    k.dma(bc1[:], bc1_i)
    k.dma(bc2[:], bc2_i)
    for t0 in range(0, TH, 512):
        k.v('memset', yac[:], 0.0)
        for grp in range(4):
            for jj in range(8):
                jc = grp * 8 + jj
                k.dma(stage[:], w_i[jc * 128:(jc + 1) * 128, :])
                k.v('tensor_copy', out=wgrp[:, jj * D:(jj + 1) * D], in_=stage[:])
                k.dma(hgrp[:, jj * 512:(jj + 1) * 512],
                      h_i[jc * 128:(jc + 1) * 128, t0:t0 + 512])
            for fb in range(8):
                k.mm([dict(out=psA[0:128, 0:512],
                           lhsT=wgrp[:, jj * D + fb * 128:jj * D + (fb + 1) * 128],
                           rhs=hgrp[:, jj * 512:(jj + 1) * 512],
                           start=(jj == 0), stop=(jj == 7)) for jj in range(8)])
                k.v('tensor_tensor', out=yac[:, fb * 512:(fb + 1) * 512],
                    in0=yac[:, fb * 512:(fb + 1) * 512], in1=psA[0:128, 0:512],
                    op=AluOpType.add)
        for fb in range(8):
            sl = yac[:, fb * 512:(fb + 1) * 512]
            k.v('tensor_scalar', out=sl, in0=sl, scalar1=bc1[:, fb:fb + 1],
                scalar2=None, op0=AluOpType.add)
            k.v('tensor_scalar', out=sl, in0=sl, scalar1=bc2[:, fb:fb + 1],
                scalar2=None, op0=AluOpType.add)
            for src_ in (x1_i, xn1_i, da_i, ds_i):
                k.dma(stage[:, 0:512], src_[fb * 128:(fb + 1) * 128, t0:t0 + 512])
                k.v('tensor_tensor', out=sl, in0=sl, in1=stage[:, 0:512],
                    op=AluOpType.add)
            k.v('tensor_copy', out=buf[:, fb * TH + t0:fb * TH + t0 + 512], in_=sl)
            k.dma(x2_o[fb * 128:(fb + 1) * 128, t0:t0 + 512], sl)

    def cb(kc, sl):
        k.dma(xn2_o[kc * 128:(kc + 1) * 128, :], sl)
    ln_fm(k, buf, TH, c, psA, psB, None, repb, out_cb=cb)
    k.done()
    return k



def build_l2a():
    k = K("l2a")
    c = consts(k, ("eye", "ones1"))
    xe_i = k.inp("xn2e", [D, 3 + TH])
    cw_i = k.inp("cwc2", [128, 8 * 4])
    cb_i = k.inp("cbc2", [128, 8])
    wp_i = k.inp("wp", [D, 128])
    bp_i = k.inp("bpc", [128, 1])
    ws_i = k.inp("ws", [D, 64])
    bs_i = k.inp("bsc", [64, 1])
    gw_i = k.inp("gwT", [D, 8])
    bt_o = k.out("Bt", [64, TH])
    ct_o = k.out("Ct", [64, TH])
    u_o = k.out("u2", [64, TH])
    r_o = k.out("Rfm", [8, TH])
    p_o = k.out("Pfm", [8, TH])
    psA = k.ps("psA", [128, 512], F32)
    psB = k.ps("psB", [128, 512], F32)
    psG = k.ps("psG", [128, 8], F32)
    psT = k.ps("psT", [8, 128], F32)
    xnb = k.sb("xnb", [128, 8 * TH], BF16)
    cvb = k.sb("cvb", [128, 8 * TH], BF16)
    stage = k.sb("stage", [128, 2051], F32)
    tmp = k.sb("tmp", [128, TH], F32)
    tmp2 = k.sb("tmp2", [128, TH], F32)
    cws = k.sb("cws2", [128, 32], F32)
    cbs = k.sb("cbs2", [128, 8], F32)
    wpb = k.sb("wpb", [128, 8 * 128], BF16)
    wsb = k.sb("wsb", [128, 8 * 64], BF16)
    gwb = k.sb("gwb", [128, 8 * 8], BF16)
    bps = k.sb("bps", [128, 1], F32)
    bss = k.sb("bss", [64, 1], F32)
    bct = k.sb("bct", [64, TH], F32)
    prb = k.sb("prb", [128, 8], F32)
    se = k.sb("se", [128, 8], F32)
    rk = k.sb("rk", [128, 8], F32)
    g1 = k.sb("g1", [128, 8], F32)
    g2 = k.sb("g2", [128, 8], F32)
    mneg = k.sb("mneg2", [128, 1], F32)
    n10 = k.sb("n10", [128, 8], F32)
    spt2 = k.sb("spt2", [128, 512], F32)
    k.dma(cws[:], cw_i)
    k.dma(cbs[:], cb_i)
    k.dma(bps[:], bp_i)
    k.dma(bss[:], bs_i)
    k.v('memset', n10[:], -10.0)
    for kc in range(8):
        k.dma(stage[:], xe_i[kc * 128:(kc + 1) * 128, :])
        k.v('tensor_copy', out=xnb[:, kc * TH:(kc + 1) * TH], in_=stage[:, 3:])
        k.v('tensor_scalar', out=tmp[:], in0=stage[:, 3:], scalar1=cws[:, kc * 4 + 3:kc * 4 + 4],
            scalar2=None, op0=AluOpType.mult)
        for j in range(3):
            k.v('scalar_tensor_tensor', out=tmp[:], in0=stage[:, j:j + TH],
                scalar=cws[:, kc * 4 + j:kc * 4 + j + 1], in1=tmp[:],
                op0=AluOpType.mult, op1=AluOpType.add)
        k.v('tensor_scalar', out=tmp[:], in0=tmp[:], scalar1=cbs[:, kc:kc + 1],
            scalar2=None, op0=AluOpType.add)
        k.v('tensor_copy', out=cvb[:, kc * TH:(kc + 1) * TH], in_=tmp[:])
        k.dma(stage[:, 0:128], wp_i[kc * 128:(kc + 1) * 128, :])
        k.v('tensor_copy', out=wpb[:, kc * 128:(kc + 1) * 128], in_=stage[:, 0:128])
        k.dma(stage[:, 0:64], ws_i[kc * 128:(kc + 1) * 128, :])
        k.v('tensor_copy', out=wsb[:, kc * 64:(kc + 1) * 64], in_=stage[:, 0:64])
        k.dma(stage[:, 0:8], gw_i[kc * 128:(kc + 1) * 128, :])
        k.v('tensor_copy', out=gwb[:, kc * 8:(kc + 1) * 8], in_=stage[:, 0:8])
    for t0 in range(0, TH, 512):
        k.mm([dict(out=psA[0:128, 0:512], lhsT=wpb[:, kc * 128:(kc + 1) * 128],
                   rhs=cvb[:, kc * TH + t0:kc * TH + t0 + 512],
                   start=(kc == 0), stop=(kc == 7)) for kc in range(8)])
        k.v('tensor_scalar', out=tmp[:, 0:512], in0=psA[0:128, 0:512],
            scalar1=bps[:, 0:1], scalar2=None, op0=AluOpType.add)
        softplus(k, tmp2[0:64, 0:512], tmp[0:64, 0:512], spt2[0:64, 0:512])
        k.dma(bt_o[:, t0:t0 + 512], tmp2[0:64, 0:512])
        k.s('activation', out=bct[0:64, t0:t0 + 512], in_=tmp[64:128, 0:512],
            func=ACT.Tanh)
    k.dma(ct_o, bct[0:64, :])
    for t0 in range(0, TH, 512):
        k.mm([dict(out=psA[0:64, 0:512], lhsT=wsb[:, kc * 64:(kc + 1) * 64],
                   rhs=xnb[:, kc * TH + t0:kc * TH + t0 + 512],
                   start=(kc == 0), stop=(kc == 7)) for kc in range(8)])
        k.v('tensor_scalar', out=bct[0:64, t0:t0 + 512], in0=psA[0:64, 0:512],
            scalar1=bss[:, 0:1], scalar2=None, op0=AluOpType.add)
    k.dma(u_o, bct[0:64, :])
    for tt in range(16):
        k.mm([dict(out=psG[0:128, 0:8],
                   lhsT=xnb[:, kc * TH + tt * 128:kc * TH + (tt + 1) * 128],
                   rhs=gwb[:, kc * 8:(kc + 1) * 8], start=(kc == 0), stop=(kc == 7))
              for kc in range(8)])
        k.v('tensor_reduce', out=mneg[:], in_=psG[0:128, 0:8], axis=AX.X,
            op=AluOpType.max)
        k.v('tensor_scalar', out=mneg[:], in0=mneg[:], scalar1=-1.0, scalar2=None,
            op0=AluOpType.mult)
        k.v('tensor_scalar', out=prb[:], in0=psG[0:128, 0:8], scalar1=mneg[:, 0:1],
            scalar2=None, op0=AluOpType.add)
        k.s('activation', out=prb[:], in_=prb[:], func=ACT.Exp)
        k.v('tensor_reduce', out=mneg[:], in_=prb[:], axis=AX.X, op=AluOpType.add)
        k.v('reciprocal', out=mneg[:], in_=mneg[:])
        k.v('tensor_scalar', out=prb[:], in0=prb[:], scalar1=mneg[:, 0:1],
            scalar2=None, op0=AluOpType.mult)
        k.v('memset', se[:], 0.0)
        k.v('memset', rk[:], 0.0)
        for e in range(8):
            pe = prb[:, e:e + 1].broadcast_to([128, 8])
            k.v('tensor_tensor', out=g1[:], in0=pe, in1=prb[:], op=AluOpType.is_gt)
            k.v('tensor_tensor', out=rk[:], in0=rk[:], in1=g1[:], op=AluOpType.add)
            k.v('tensor_tensor', out=g1[:], in0=g1[:], in1=pe, op=AluOpType.mult)
            k.v('tensor_tensor', out=se[:], in0=se[:], in1=g1[:], op=AluOpType.add)
        k.v('tensor_scalar', out=g1[:], in0=se[:], scalar1=0.9, scalar2=None,
            op0=AluOpType.is_lt)
        k.v('tensor_scalar', out=rk[:], in0=rk[:], scalar1=1.0, scalar2=None,
            op0=AluOpType.add)
        k.v('tensor_tensor', out=g2[:], in0=prb[:], in1=rk[:], op=AluOpType.subtract)
        k.v('tensor_scalar', out=g2[:], in0=g2[:], scalar1=10.0, scalar2=None,
            op0=AluOpType.add)
        k.v('tensor_tensor', out=g2[:], in0=g2[:], in1=g1[:], op=AluOpType.mult)
        k.v('tensor_scalar', out=g2[:], in0=g2[:], scalar1=-10.0, scalar2=None,
            op0=AluOpType.add)
        k.tr([(psT[0:8, 0:128], g2[:], c["eye"][:])])
        k.v('tensor_copy', out=tmp[0:8, tt * 128:(tt + 1) * 128], in_=psT[0:8, 0:128])
        k.tr([(psT[0:8, 0:128], prb[:], c["eye"][:])])
        k.v('tensor_copy', out=tmp2[0:8, tt * 128:(tt + 1) * 128], in_=psT[0:8, 0:128])
    k.dma(r_o, tmp[0:8, :])
    k.dma(p_o, tmp2[0:8, :])
    k.done()
    return k


def build_l2scan():
    k = K("l2scan")
    br_i = k.inp("Br", [32, T])
    cr_i = k.inp("Cr", [32, T])
    ur_i = k.inp("ur", [32, T])
    al_i = k.inp("al32", [32, 1])
    dl_i = k.inp("dl32", [32, 1])
    y_o = k.out("y2", [32, T])
    br = k.sb("br_s", [32, T], F32)
    cr = k.sb("cr_s", [32, T], F32)
    ur = k.sb("ur_s", [32, T], F32)
    av = k.sb("av", [32, 1], F32)
    dv = k.sb("dv", [32, 1], F32)
    dec = k.sb("dec", [32, 1], F32)
    d0 = k.sb("d0l", [32, T], F32)
    hh = k.sb("hhl", [32, T], F32)
    spt3 = k.sb("spt3", [32, 1], F32)
    k.dma(br[:], br_i)
    k.dma(cr[:], cr_i)
    k.dma(ur[:], ur_i)
    k.dma(av[:], al_i)
    k.dma(dv[:], dl_i)
    softplus(k, av[:], av[:], spt3[:])
    k.v('tensor_scalar', out=av[:], in0=av[:], scalar1=-1.0, scalar2=None,
        op0=AluOpType.mult)
    softplus(k, dv[:], dv[:], spt3[:])
    k.v('tensor_tensor', out=dec[:], in0=av[:], in1=dv[:], op=AluOpType.mult)
    k.v('tensor_scalar', out=dec[:], in0=dec[:], scalar1=1.0, scalar2=None,
        op0=AluOpType.add)
    k.v('memset', d0[:], 1.0)
    k.v('tensor_scalar', out=d0[:], in0=d0[:], scalar1=dec[:, 0:1], scalar2=None,
        op0=AluOpType.mult)
    k.v('tensor_tensor', out=ur[:], in0=ur[:], in1=br[:], op=AluOpType.mult)
    k.v('tensor_scalar', out=ur[:], in0=ur[:], scalar1=dv[:, 0:1], scalar2=None,
        op0=AluOpType.mult)
    scan_seg(k, hh, d0, ur, T)
    k.v('tensor_tensor', out=hh[:], in0=hh[:], in1=cr[:], op=AluOpType.mult)
    k.dma(y_o, hh[:])
    k.done()
    return k


def build_l2thresh():
    k = K("l2thresh")
    r_i = k.inp("RT", [8, 4 * T])
    thr_o = k.out("thr", [8, 1])
    rt = k.sb("rt", [8, 4 * T], F32)
    lo = k.sb("lo2", [8, 1], F32)
    hi = k.sb("hi2", [8, 1], F32)
    mid = k.sb("mid2", [8, 1], F32)
    cm = k.sb("cm2", [8, 4 * T], F32)
    cnt = k.sb("cnt2", [8, 1], F32)
    ge = k.sb("ge2", [8, 1], F32)
    k.dma(rt[:], r_i)
    k.v('memset', lo[:], -11.0)
    k.v('memset', hi[:], 3.0)
    for _ in range(55):
        k.v('tensor_tensor', out=mid[:], in0=lo[:], in1=hi[:], op=AluOpType.add)
        k.v('tensor_scalar', out=mid[:], in0=mid[:], scalar1=0.5, scalar2=None,
            op0=AluOpType.mult)
        k.v('tensor_scalar', out=cm[:], in0=rt[:], scalar1=mid[:, 0:1], scalar2=None,
            op0=AluOpType.is_gt)
        k.v('tensor_reduce', out=cnt[:], in_=cm[:], axis=AX.X, op=AluOpType.add)
        k.v('tensor_scalar', out=ge[:], in0=cnt[:], scalar1=2048.0, scalar2=None,
            op0=AluOpType.is_ge)
        k.v('tensor_tensor', out=cnt[:], in0=mid[:], in1=lo[:], op=AluOpType.subtract)
        k.v('tensor_tensor', out=cnt[:], in0=cnt[:], in1=ge[:], op=AluOpType.mult)
        k.v('tensor_tensor', out=lo[:], in0=lo[:], in1=cnt[:], op=AluOpType.add)
        k.v('tensor_scalar', out=ge[:], in0=ge[:], scalar1=-1.0, scalar2=1.0,
            op0=AluOpType.mult, op1=AluOpType.add)
        k.v('tensor_tensor', out=cnt[:], in0=mid[:], in1=hi[:], op=AluOpType.subtract)
        k.v('tensor_tensor', out=cnt[:], in0=cnt[:], in1=ge[:], op=AluOpType.mult)
        k.v('tensor_tensor', out=hi[:], in0=hi[:], in1=cnt[:], op=AluOpType.add)
    k.dma(thr_o, lo[:])
    k.done()
    return k


def build_l2moe():
    k = K("l2moe")
    c = consts(k, ("ones1", "e8rep"))
    xn_i = k.inp("xn2", [D, TH])
    w1_i = k.inp("w1T", [8, D, D])
    w2_i = k.inp("w2T", [8, D, D])
    r_i = k.inp("Rfm", [8, TH])
    p_i = k.inp("Pfm", [8, TH])
    thr_i = k.inp("thr", [8, 1])
    y_o = k.out("ymoe", [D, TH])
    psA = k.ps("psA", [128, 512], F32)
    psB = k.ps("psB", [128, 512], F32)
    xnb = k.sb("xnb", [128, 8 * TH], BF16)
    hbf = k.sb("hbf", [128, 8 * TH], BF16)
    yac = k.sb("yac", [128, 8 * TH], F32)
    stage = k.sb("stage", [128, 2048], F32)
    wbf = k.sb("wbf", [128, 8 * D], BF16)
    sel = k.sb("sel", [8, TH], F32)
    selr = k.sb("selr", [128, TH], F32)
    thr = k.sb("thr_s", [8, 1], F32)
    t1 = k.sb("t1m", [8, TH], F32)
    k.dma(thr[:], thr_i)
    k.dma(sel[:], r_i)
    k.dma(t1[:], p_i)
    k.v('memset', yac[:], 0.0)
    k.v('tensor_scalar', out=selr[0:8, 0:TH], in0=sel[:], scalar1=thr[:, 0:1],
        scalar2=None, op0=AluOpType.is_gt)
    k.v('tensor_scalar', out=sel[:], in0=sel[:], scalar1=-9.5, scalar2=None,
        op0=AluOpType.is_gt)
    k.v('tensor_tensor', out=sel[:], in0=sel[:], in1=selr[0:8, 0:TH],
        op=AluOpType.mult)
    k.v('tensor_tensor', out=sel[:], in0=sel[:], in1=t1[:], op=AluOpType.mult)
    for kc in range(8):
        k.dma(stage[:], xn_i[kc * 128:(kc + 1) * 128, :])
        k.v('tensor_copy', out=xnb[:, kc * TH:(kc + 1) * TH], in_=stage[:])
    for e in range(8):
        for kc in range(8):
            k.dma(stage[:, 0:D], w1_i[e, kc * 128:(kc + 1) * 128, :])
            k.v('tensor_copy', out=wbf[:, kc * D:(kc + 1) * D], in_=stage[:, 0:D])

        def ev_h(m0, mw, t0, tw, ps):
            k.s('activation', out=stage[:, 0:tw], in_=ps, func=ACT.Gelu)
            k.v('tensor_copy',
                out=hbf[:, (m0 // 128) * TH + t0:(m0 // 128) * TH + t0 + tw],
                in_=stage[:, 0:tw])
        mm_fm(k, lambda kc, m0, mw: wbf[:, kc * D + m0:kc * D + m0 + mw],
              lambda kc, t0, tw: xnb[:, kc * TH + t0:kc * TH + t0 + tw],
              D, D, TH, ev_h, psA, psB)
        for t0 in range(0, TH, 512):
            k.mm([dict(out=psB[0:128, 0:512], lhsT=c["e8rep"][:, e * 128:(e + 1) * 128],
                       rhs=sel[:, t0:t0 + 512], start=True, stop=True)])
            k.v('tensor_copy', out=selr[:, t0:t0 + 512], in_=psB[0:128, 0:512])
        for kc in range(8):
            k.dma(stage[:, 0:D], w2_i[e, kc * 128:(kc + 1) * 128, :])
            k.v('tensor_copy', out=wbf[:, kc * D:(kc + 1) * D], in_=stage[:, 0:D])

        def ev_y(m0, mw, t0, tw, ps):
            k.v('tensor_tensor', out=stage[:, 0:tw], in0=ps, in1=selr[:, t0:t0 + tw],
                op=AluOpType.mult)
            sl = yac[:, (m0 // 128) * TH + t0:(m0 // 128) * TH + t0 + tw]
            k.v('tensor_tensor', out=sl, in0=sl, in1=stage[:, 0:tw], op=AluOpType.add)
        mm_fm(k, lambda kc, m0, mw: wbf[:, kc * D + m0:kc * D + m0 + mw],
              lambda kc, t0, tw: hbf[:, kc * TH + t0:kc * TH + t0 + tw],
              D, D, TH, ev_y, psA, psB)
    for kc in range(8):
        k.dma(y_o[kc * 128:(kc + 1) * 128, :], yac[:, kc * TH:(kc + 1) * TH])
    k.done()
    return k


def build_comb2():
    k = K("comb2")
    c = consts(k, ("eye", "ones1", "ones128"))
    x2_i = k.inp("x2", [D, TH])
    ym_i = k.inp("ymoe", [D, TH])
    yl_i = k.inp("ylin", [64, TH])
    wo_i = k.inp("wo2", [64, D])
    bo_i = k.inp("boc2", [128, 8])
    wn_i = k.inp("wnc", [128, 8])
    x3_o = k.out("x3", [D, TH])
    xn3_o = k.out("xn3", [D, TH])
    xr_o = k.out("xrmsb", [D, TH], BF16)
    psA = k.ps("psA", [128, 512], F32)
    psB = k.ps("psB", [128, 512], F32)
    buf = k.sb("buf", [128, 8 * TH], F32)
    repb = k.sb("repb", [128, 2 * TH], F32)
    stage = k.sb("stage", [128, 2048], F32)
    zc = k.sb("zc", [128, 1], F32)
    ylb = k.sb("ylb", [64, TH], BF16)
    wob = k.sb("wob", [64, D], BF16)
    boc = k.sb("boc_s", [128, 8], F32)
    wnc = k.sb("wnc_s", [128, 8], F32)
    tb = k.sb("tb", [128, TH], BF16)
    k.dma(boc[:], bo_i)
    k.dma(wnc[:], wn_i)
    k.v('memset', zc[:], 0.0)
    k.dma(stage[0:64, 0:TH], yl_i)
    k.v('tensor_copy', out=ylb[:], in_=stage[0:64, 0:TH])
    k.dma(stage[0:64, 0:D], wo_i)
    k.v('tensor_copy', out=wob[:], in_=stage[0:64, 0:D])
    for fb in range(8):
        for t0 in range(0, TH, 512):
            k.mm([dict(out=psA[0:128, 0:512], lhsT=wob[:, fb * 128:(fb + 1) * 128],
                       rhs=ylb[:, t0:t0 + 512], start=True, stop=True)])
            sl = buf[:, fb * TH + t0:fb * TH + t0 + 512]
            k.v('tensor_scalar', out=sl, in0=psA[0:128, 0:512],
                scalar1=boc[:, fb:fb + 1], scalar2=None, op0=AluOpType.add)
            for src_ in (x2_i, ym_i):
                k.dma(stage[:, 0:512], src_[fb * 128:(fb + 1) * 128, t0:t0 + 512])
                k.v('tensor_tensor', out=sl, in0=sl, in1=stage[:, 0:512],
                    op=AluOpType.add)
        k.dma(x3_o[fb * 128:(fb + 1) * 128, :], buf[:, fb * TH:(fb + 1) * TH])

    def cb1(kc, sl):
        k.dma(xn3_o[kc * 128:(kc + 1) * 128, :], sl)
    ln_fm(k, buf, TH, c, psA, psB, None, repb, out_cb=cb1)

    def cb2(kc, sl):
        k.v('tensor_copy', out=tb[:], in_=sl)
        k.dma(xr_o[kc * 128:(kc + 1) * 128, :], tb[:])
    ln_fm(k, buf, TH, c, psA, psB, None, repb, rms=True, eps=1e-6,
          affine=(lambda kc: wnc[:, kc:kc + 1],
                  lambda kc: zc[:, 0:1]), out_cb=cb2)
    k.done()
    return k



def build_l3a():
    k = K("l3a")
    c = consts(k, ("ones1",))
    xr_i = k.inp("xrmsb", [D, TH], BF16)
    wi_i = k.inp("win", [D, 3 * D])
    lam_i = k.inp("lamc", [128, 8])
    a_o = k.out("afm", [D, TH])
    u_o = k.out("ufm", [D, TH])
    psA = k.ps("psA", [128, 512], F32)
    psB = k.ps("psB", [128, 512], F32)
    psC = k.ps("psC2", [128, 512], F32)
    xb = k.sb("xb", [128, 8 * TH], BF16)
    stage = k.sb("stage", [128, 3 * D], F32)
    wbf = k.sb("wbf", [128, 8 * 3 * D], BF16)
    lam = k.sb("lam_s", [128, 8], F32)
    at = k.sb("at", [128, 512], F32)
    sg = k.sb("sg", [128, 512], F32)
    gv = k.sb("gv", [128, 512], F32)
    k.dma(lam[:], lam_i)
    k.s('activation', out=lam[:], in_=lam[:], func=ACT.Sigmoid)
    for kc in range(8):
        k.dma(xb[:, kc * TH:(kc + 1) * TH], xr_i[kc * 128:(kc + 1) * 128, :])
    load_w(k, wi_i, wbf, D, 3 * D, stage)
    W3 = 3 * D
    for mc in range(8):
        for t0 in range(0, TH, 512):
            for ps, off in ((psA, 0), (psB, D), (psC, 2 * D)):
                k.mm([dict(out=ps[0:128, 0:512],
                           lhsT=wbf[:, kc * W3 + off + mc * 128:
                                    kc * W3 + off + (mc + 1) * 128],
                           rhs=xb[:, kc * TH + t0:kc * TH + t0 + 512],
                           start=(kc == 0), stop=(kc == 7)) for kc in range(8)])
            k.s('activation', out=at[:], in_=psA[0:128, 0:512], func=ACT.Sigmoid)
            k.dma(a_o[mc * 128:(mc + 1) * 128, t0:t0 + 512], at[:])
            k.s('activation', out=sg[:], in_=psB[0:128, 0:512], func=ACT.Sigmoid)
            k.s('activation', out=gv[:], in_=psC[0:128, 0:512], func=ACT.Gelu)
            k.v('tensor_scalar', out=at[:], in0=at[:], scalar1=-1.0, scalar2=1.0,
                op0=AluOpType.mult, op1=AluOpType.add)
            k.v('tensor_tensor', out=at[:], in0=at[:], in1=sg[:], op=AluOpType.mult)
            k.v('tensor_tensor', out=at[:], in0=at[:], in1=gv[:], op=AluOpType.mult)
            k.v('tensor_scalar', out=at[:], in0=at[:], scalar1=lam[:, mc:mc + 1],
                scalar2=None, op0=AluOpType.mult)
            k.dma(u_o[mc * 128:(mc + 1) * 128, t0:t0 + 512], at[:])
    k.done()
    return k


def build_l3scan():
    k = K("l3scan")
    a_i = k.inp("asl", [128, 4 * T])
    u_i = k.inp("usl", [128, 4 * T])
    y_o = k.out("y3", [128, 4 * T])
    av = k.sb("av3", [128, 4 * T], F32)
    uv = k.sb("uv3", [128, 4 * T], F32)
    yv = k.sb("yv3", [128, 4 * T], F32)
    k.dma(av[:], a_i)
    k.dma(uv[:], u_i)
    for b in range(4):
        scan_seg(k, yv[:, b * T:(b + 1) * T], av[:, b * T:(b + 1) * T],
                 uv[:, b * T:(b + 1) * T], T)
    k.dma(y_o, yv[:])
    k.done()
    return k


def build_l3final():
    k = K("l3final")
    c = consts(k, ("eye", "ones1", "e8rep"))
    x3_i = k.inp("x3", [D, TH])
    xn3_i = k.inp("xn3", [D, TH])
    y_i = k.inp("yfm", [D, TH])
    wo_i = k.inp("wout", [D, D])
    gw_i = k.inp("gwT3", [D, 8])
    gb_i = k.inp("gbrep", [128, 8])
    ew_i = k.inp("ewT", [8, D, D])
    eb_i = k.inp("ebc", [128, 64])
    out_o = k.out("outf", [D, TH])
    psA = k.ps("psA", [128, 512], F32)
    psB = k.ps("psB", [128, 512], F32)
    psG = k.ps("psG3", [128, 8], F32)
    psT = k.ps("psT3", [8, 128], F32)
    buf = k.sb("buf", [128, 8 * TH], F32)
    xnb = k.sb("xnb", [128, 8 * TH], BF16)
    ybf = k.sb("ybf", [128, 8 * TH], BF16)
    stage = k.sb("stage", [128, 2048], F32)
    wbf = k.sb("wbf", [128, 8 * D], BF16)
    gwb = k.sb("gwb3", [128, 8 * 8], BF16)
    gbr = k.sb("gbr", [128, 8], F32)
    ebc = k.sb("ebc_s", [128, 8 * 8], F32)
    wfm = k.sb("wfm", [8, TH], F32)
    wrep = k.sb("wrep", [128, TH], F32)
    g1 = k.sb("g13", [128, 8], F32)
    g2 = k.sb("g23", [128, 8], F32)
    mneg = k.sb("mneg3", [128, 1], F32)
    k.dma(gbr[:], gb_i)
    k.dma(ebc[:], eb_i)
    for kc in range(8):
        k.dma(stage[:], xn3_i[kc * 128:(kc + 1) * 128, :])
        k.v('tensor_copy', out=xnb[:, kc * TH:(kc + 1) * TH], in_=stage[:])
        k.dma(stage[:], y_i[kc * 128:(kc + 1) * 128, :])
        k.v('tensor_copy', out=ybf[:, kc * TH:(kc + 1) * TH], in_=stage[:])
        k.dma(stage[:, 0:8], gw_i[kc * 128:(kc + 1) * 128, :])
        k.v('tensor_copy', out=gwb[:, kc * 8:(kc + 1) * 8], in_=stage[:, 0:8])
    load_w(k, wo_i, wbf, D, D, stage[:, 0:1024])

    def ev0(m0, mw, t0, tw, ps):
        sl = buf[:, (m0 // 128) * TH + t0:(m0 // 128) * TH + t0 + tw]
        k.v('tensor_copy', out=sl, in_=ps)
        for src_ in (x3_i, xn3_i):
            k.dma(stage[:, 0:tw], src_[m0:m0 + mw, t0:t0 + tw])
            k.v('tensor_tensor', out=sl, in0=sl, in1=stage[:, 0:tw], op=AluOpType.add)
    mm_fm(k, lambda kc, m0, mw: wbf[:, kc * D + m0:kc * D + m0 + mw],
          lambda kc, t0, tw: ybf[:, kc * TH + t0:kc * TH + t0 + tw],
          D, D, TH, ev0, psA, psB)
    for tt in range(16):
        k.mm([dict(out=psG[0:128, 0:8],
                   lhsT=xnb[:, kc * TH + tt * 128:kc * TH + (tt + 1) * 128],
                   rhs=gwb[:, kc * 8:(kc + 1) * 8], start=(kc == 0), stop=(kc == 7))
              for kc in range(8)])
        k.v('tensor_tensor', out=g1[:], in0=psG[0:128, 0:8], in1=gbr[:],
            op=AluOpType.add)
        k.v('tensor_reduce', out=mneg[:], in_=g1[:], axis=AX.X, op=AluOpType.max)
        k.v('tensor_scalar', out=mneg[:], in0=mneg[:], scalar1=-1.0, scalar2=None,
            op0=AluOpType.mult)
        k.v('tensor_scalar', out=g1[:], in0=g1[:], scalar1=mneg[:, 0:1], scalar2=None,
            op0=AluOpType.add)
        k.s('activation', out=g1[:], in_=g1[:], func=ACT.Exp)
        k.v('tensor_reduce', out=mneg[:], in_=g1[:], axis=AX.X, op=AluOpType.add)
        k.v('reciprocal', out=mneg[:], in_=mneg[:])
        k.v('tensor_scalar', out=g1[:], in0=g1[:], scalar1=mneg[:, 0:1], scalar2=None,
            op0=AluOpType.mult)
        k.v('tensor_reduce', out=mneg[:], in_=g1[:], axis=AX.X, op=AluOpType.max)
        k.v('tensor_scalar', out=g2[:], in0=g1[:], scalar1=mneg[:, 0:1], scalar2=None,
            op0=AluOpType.is_ge)
        k.v('tensor_scalar', out=g2[:], in0=g2[:], scalar1=-1e9, scalar2=None,
            op0=AluOpType.mult)
        k.v('tensor_tensor', out=g2[:], in0=g1[:], in1=g2[:], op=AluOpType.add)
        k.v('tensor_reduce', out=mneg[:], in_=g2[:], axis=AX.X, op=AluOpType.max)
        k.v('tensor_scalar', out=g2[:], in0=g1[:], scalar1=mneg[:, 0:1], scalar2=None,
            op0=AluOpType.is_ge)
        k.v('tensor_tensor', out=g1[:], in0=g1[:], in1=g2[:], op=AluOpType.mult)
        k.tr([(psT[0:8, 0:128], g1[:], c["eye"][:])])
        k.v('tensor_copy', out=wfm[0:8, tt * 128:(tt + 1) * 128], in_=psT[0:8, 0:128])
    for e in range(8):
        for kc in range(8):
            k.dma(stage[:, 0:D], ew_i[e, kc * 128:(kc + 1) * 128, :])
            k.v('tensor_copy', out=wbf[:, kc * D:(kc + 1) * D], in_=stage[:, 0:D])
        for t0 in range(0, TH, 512):
            k.mm([dict(out=psB[0:128, 0:512], lhsT=c["e8rep"][:, e * 128:(e + 1) * 128],
                       rhs=wfm[:, t0:t0 + 512], start=True, stop=True)])
            k.v('tensor_copy', out=wrep[:, t0:t0 + 512], in_=psB[0:128, 0:512])

        def ev_e(m0, mw, t0, tw, ps, e=e):
            k.v('tensor_scalar', out=stage[:, 0:tw], in0=ps,
                scalar1=ebc[:, e * 8 + m0 // 128:e * 8 + m0 // 128 + 1],
                scalar2=None, op0=AluOpType.add)
            k.v('tensor_tensor', out=stage[:, 0:tw], in0=stage[:, 0:tw],
                in1=wrep[:, t0:t0 + tw], op=AluOpType.mult)
            sl = buf[:, (m0 // 128) * TH + t0:(m0 // 128) * TH + t0 + tw]
            k.v('tensor_tensor', out=sl, in0=sl, in1=stage[:, 0:tw], op=AluOpType.add)
        mm_fm(k, lambda kc, m0, mw: wbf[:, kc * D + m0:kc * D + m0 + mw],
              lambda kc, t0, tw: xnb[:, kc * TH + t0:kc * TH + t0 + tw],
              D, D, TH, ev_e, psA, psB)
    for kc in range(8):
        k.dma(out_o[kc * 128:(kc + 1) * 128, :], buf[:, kc * TH:(kc + 1) * TH])
    k.done()
    return k


def _cols(v, n=8):
    a = np.asarray(v)
    return np.ascontiguousarray(a.reshape(n, 128).T)


def _t(a):
    return np.ascontiguousarray(np.asarray(a).T)


def _l0(ip):
    x = np.asarray(ip["L0_attn_x"], np.float32)
    xf = [np.ascontiguousarray(x[b].T) for b in range(B)]
    wipT = _t(np.asarray(ip["L0_attn_in_proj_w"], np.float32))
    cw = np.asarray(ip["L0_attn_conv_w"], np.float32)[:, 0, :]
    cwc = np.ascontiguousarray(cw.reshape(8, 128, 4).transpose(1, 0, 2).reshape(128, 32))
    cbc = _cols(np.asarray(ip["L0_attn_conv_b"], np.float32))
    f8k = np.ascontiguousarray(np.asarray(ip["L0_ffn_kernel"]))
    fbc = np.ascontiguousarray(np.asarray(ip["L0_ffn_bias"]).reshape(8, 128).T)
    s_in = np.asarray(ip["L0_ffn_input_scale"], np.float32).reshape(1, 1)
    s_k = np.asarray(ip["L0_ffn_kernel_scale"], np.float32).reshape(1, 1)
    maps = []
    for c_ in range(8):
        b, h = divmod(c_, 2)
        xe = np.zeros((D, EXT), np.float32)
        xe[:, 128:] = xf[b][:, h * TH:(h + 1) * TH]
        if h == 1:
            xe[:, 125:128] = xf[b][:, TH - 3:TH]
        maps.append(dict(xext=xe, wipT=wipT, cwc=cwc, cbc=cbc, f8k=f8k,
                         s_in=s_in, s_k=s_k, fbc=fbc))
    ra = run("pre_a", build_pre_a, maps)

    maps = []
    for c_ in range(8):
        maps.append(dict(xp=np.asarray(ra[c_]["xp"]),
                         wqT=_t(np.asarray(ip["L0_attn_W_q"], np.float32)),
                         wkT=_t(np.asarray(ip["L0_attn_W_k"], np.float32)),
                         wvT=_t(np.asarray(ip["L0_attn_W_v"], np.float32)),
                         waT=_t(np.asarray(ip["L0_attn_W_alpha"], np.float32)),
                         wbT=_t(np.asarray(ip["L0_attn_W_beta"], np.float32)),
                         bac=_cols(np.asarray(ip["L0_attn_b_alpha"], np.float32)),
                         bbc=_cols(np.asarray(ip["L0_attn_b_beta"], np.float32))))
    rb = run("pre_b", build_pre_b, maps)

    def full(nm):
        return [np.concatenate([rb[2 * b][nm], rb[2 * b + 1][nm]], axis=1)
                for b in range(B)]
    qf, kf, vf, laf, bef = full("q"), full("k"), full("v"), full("loga"), full("beta")
    maps = []
    for c_ in range(8):
        gs = list(range(8 * c_, 8 * c_ + 8))
        qem = np.concatenate(
            [qf[g // 16][(g % 16) * 64:(g % 16 + 1) * 64, :] for g in gs], axis=1)
        kem = np.concatenate(
            [kf[g // 16][(g % 16) * 64:(g % 16 + 1) * 64, :] for g in gs], axis=1)

        def tm(src, g):
            sl = src[g // 16][(g % 16) * 64:(g % 16 + 1) * 64, :]
            return sl.T.reshape(32, 128, 64).transpose(1, 0, 2).reshape(128, 32 * 64)
        ktm = np.concatenate([tm(kf, g) for g in gs], axis=1)
        vtm = np.concatenate([tm(vf, g) for g in gs], axis=1)
        la = np.concatenate(
            [laf[g // 16][g % 16].reshape(32, 128).T for g in gs], axis=1)
        be = np.concatenate(
            [bef[g // 16][g % 16].reshape(32, 128).T for g in gs], axis=1)
        maps.append(dict(qem=np.ascontiguousarray(qem), kem=np.ascontiguousarray(kem),
                         ktm=np.ascontiguousarray(ktm), vtm=np.ascontiguousarray(vtm),
                         la=np.ascontiguousarray(la), be=np.ascontiguousarray(be)))
    rs = run("scan0", build_scan0, maps)
    cellfm = [np.zeros((D, T), np.float32) for _ in range(B)]
    for c_ in range(8):
        for j, g in enumerate(range(8 * c_, 8 * c_ + 8)):
            b, h = g // 16, g % 16
            cellfm[b][h * 64:(h + 1) * 64, :] = rs[c_]["cell"][j].T

    maps = []
    for c_ in range(8):
        b, h = divmod(c_, 2)
        maps.append(dict(
            x=np.ascontiguousarray(xf[b][:, h * TH:(h + 1) * TH]),
            cell=np.ascontiguousarray(cellfm[b][:, h * TH:(h + 1) * TH]),
            ffn0=np.asarray(ra[c_]["ffn0"]),
            opT=_t(np.asarray(ip["L0_attn_out_proj_w"], np.float32)),
            rw1=np.asarray(ip["L1_attn_r_w1"], np.float32),
            rw2=np.ascontiguousarray(
                np.asarray(ip["L1_attn_r_w2"], np.float32).reshape(4, 128).T),
            rb1c=_cols(np.asarray(ip["L1_attn_r_b1"], np.float32), 4),
            rb2=np.asarray(ip["L1_attn_r_b2"], np.float32).reshape(1, 1),
            lnwc=_cols(np.asarray(ip["L1_attn_ln_w"], np.float32)),
            lnbc=_cols(np.asarray(ip["L1_attn_ln_b"], np.float32))))
    rp = run("post0", build_post0, maps)
    scores = np.zeros((B, T), np.float32)
    for c_ in range(8):
        b, h = divmod(c_, 2)
        scores[b, h * TH:(h + 1) * TH] = rp[c_]["scores"][0]
    return dict(ra=ra, rb=rb, rs=rs, rp=rp, scores=scores, xf=xf)


def _bf_vecs(weights, transpose):
    out = np.zeros((12, 4, 4096), np.float32)
    i = np.arange(4096)
    for s in range(12):
        st = 1 << s
        nb_i = i // (2 * st)
        r = i % (2 * st)
        half = r // st
        rr = r % st
        kidx = nb_i * st + rr
        w = np.asarray(weights[s], np.float32)
        w00 = w[kidx, half, half]
        w01 = w[kidx, half, 1 - half]
        if transpose:
            w01 = w01[i ^ st]
        w00r = w00.reshape(-1, 2, st)
        w01r = w01.reshape(-1, 2, st)
        out[s, 0, :2048] = w00r[:, 0, :].ravel()
        out[s, 1, :2048] = w01r[:, 0, :].ravel()
        out[s, 2, :2048] = w00r[:, 1, :].ravel()
        out[s, 3, :2048] = w01r[:, 1, :].ravel()
    return out


def _l123(ip, r0):
    f32 = np.float32
    A = lambda x: np.asarray(x, f32)
    C = np.ascontiguousarray
    rp = r0["rp"]
    xf = r0["xf"]
    wexp = A(ip["L1_ffn_W_exp"])
    wcon = A(ip["L1_ffn_W_con"])
    uw = _bf_vecs(ip["L1_ffn_up_weights"], False)
    dw = _bf_vecs(ip["L1_ffn_down_weights"], True)
    maps = []
    for c_ in range(8):
        maps.append(dict(scores=r0["scores"], pup=C(wexp[c_ * 128:(c_ + 1) * 128, :]),
                         pcn=C(wcon[:, c_ * 128:(c_ + 1) * 128].T), uw=uw, dw=dw,
                         brow=A(ip["L1_ffn_b_exp"]).reshape(1, 4096),
                         ubias=A(ip["L1_ffn_up_bias"]).reshape(1, 4096)))
    rm = run("misc1", build_misc1, maps)
    mask = rm[0]["mask"] > 0.5
    wupf = np.concatenate([rm[c_]["wup"] for c_ in range(8)], axis=0)
    wcnT = C(np.concatenate([rm[c_]["wcn"] for c_ in range(8)], axis=0).T)
    btupc = C(rm[0]["btup"].reshape(32, 128).T)
    sel, non = [], []
    for b in range(B):
        s_ = np.nonzero(mask[b])[0]
        n_ = np.nonzero(~mask[b])[0]
        if len(s_) != 1024:
            allidx = np.concatenate([s_, n_])
            s_ = np.sort(allidx[:1024])
            n_ = np.sort(allidx[1024:])
        sel.append(s_)
        non.append(n_)
    xn1i = [np.concatenate([rp[2 * b]["xn1i"], rp[2 * b + 1]["xn1i"]], axis=1)
            for b in range(B)]
    wq, wk, wv = A(ip["L1_attn_Wq"]), A(ip["L1_attn_Wk"]), A(ip["L1_attn_Wv"])
    bq, bk, bv = A(ip["L1_attn_bq"]), A(ip["L1_attn_bk"]), A(ip["L1_attn_bv"])
    maps = []
    for c_ in range(8):
        b, hh = divmod(c_, 2)
        sl = slice(hh * 512, (hh + 1) * 512)
        maps.append(dict(xat=C(xn1i[b][:, sel[b]]),
                         wq=C(wq[:, sl]), wk=C(wk[:, sl]), wv=C(wv[:, sl]),
                         bq=C(bq[sl].reshape(8, 64).T), bk=C(bk[sl].reshape(8, 64).T),
                         bv=C(bv[sl].reshape(1, 512))))
    rat = run("attn1", build_attn1, maps)
    ofm = []
    for b in range(B):
        of = np.zeros((1024, 1024), f32)
        for hh in range(2):
            for h in range(8):
                of[:, hh * 512 + h * 64:hh * 512 + (h + 1) * 64] = rat[2 * b + hh]["o"][h]
        ofm.append(C(of.T))
    maps = []
    for c_ in range(8):
        b, hf = divmod(c_, 2)
        maps.append(dict(ofm=C(ofm[b][:, hf * 512:(hf + 1) * 512]),
                         wo=A(ip["L1_attn_Wo"]), boc=_cols(A(ip["L1_attn_bo"]))))
    rgs = run("gemm_small", build_gemm_small, maps)
    dattn = [np.zeros((D, T), f32) for _ in range(B)]
    for c_ in range(8):
        b, hf = divmod(c_, 2)
        dattn[b][:, sel[b][hf * 512:(hf + 1) * 512]] = rgs[c_]["dattn"]
    xs_all = np.concatenate([xn1i[b][:, non[b]] for b in range(B)], axis=1)
    maps = [dict(xs=C(xs_all[:, c_ * 1536:(c_ + 1) * 1536]),
                 inw=A(ip["L1_attn_in_w"])) for c_ in range(8)]
    rsi = run("ssm_in", build_ssm_in, maps)
    xi_all = np.concatenate([rsi[c_]["xi"] for c_ in range(8)], axis=1)
    z_all = np.concatenate([rsi[c_]["z"] for c_ in range(8)], axis=1)
    cw1 = A(ip["L1_attn_conv_w"])
    cb1 = A(ip["L1_attn_conv_b"])
    maps = []
    for c_ in range(8):
        rs = slice(c_ * 256, (c_ + 1) * 256)
        maps.append(dict(
            xisl=C(xi_all[rs]),
            cwsl=C(cw1[rs].reshape(2, 128, 4).transpose(1, 0, 2).reshape(128, 8)),
            cbsl=C(cb1[rs].reshape(2, 128).T)))
    rsc = run("ssm_conv", build_ssm_conv, maps)
    xc_all = np.concatenate([rsc[c_]["xc"] for c_ in range(8)], axis=0)
    maps = [dict(xcs=C(xc_all[:, c_ * 1536:(c_ + 1) * 1536]),
                 xprojw=A(ip["L1_attn_xproj_w"]), dtw=A(ip["L1_attn_dt_w"]),
                 dtbc=_cols(A(ip["L1_attn_dt_b"]), 16)) for c_ in range(8)]
    rsp = run("ssm_proj", build_ssm_proj, maps)
    de_all = np.concatenate([rsp[c_]["delta"] for c_ in range(8)], axis=1)
    dx_all = np.concatenate([rsp[c_]["dx"] for c_ in range(8)], axis=1)
    bm_all = np.concatenate([rsp[c_]["bm"] for c_ in range(8)], axis=1)
    cm_all = np.concatenate([rsp[c_]["cm"] for c_ in range(8)], axis=1)
    alog = A(ip["L1_attn_A_log"])
    maps = []
    for c_ in range(8):
        rs = slice(c_ * 256, (c_ + 1) * 256)
        asl = alog[rs].reshape(32, 8, 16)
        alane = np.stack([asl[db].T.ravel() for db in range(32)], axis=1)
        maps.append(dict(des=C(de_all[rs]), dxs=C(dx_all[rs]), bms=bm_all,
                         cms=cm_all, alane=C(alane.astype(f32))))
    rss = run("ssm_scan", build_ssm_scan, maps)
    y_all = np.concatenate([rss[c_]["y"] for c_ in range(8)], axis=0)
    maps = [dict(ys=C(y_all[:, c_ * 1536:(c_ + 1) * 1536]),
                 xcs2=C(xc_all[:, c_ * 1536:(c_ + 1) * 1536]),
                 zs=C(z_all[:, c_ * 1536:(c_ + 1) * 1536]),
                 outw=A(ip["L1_attn_out_w"]), dpc=_cols(A(ip["L1_attn_Dp"]), 16))
            for c_ in range(8)]
    rsg = run("ssm_gate", build_ssm_gate, maps)
    dssm_all = np.concatenate([rsg[c_]["dssm"] for c_ in range(8)], axis=1)
    dssm = [np.zeros((D, T), f32) for _ in range(B)]
    for b in range(B):
        dssm[b][:, non[b]] = dssm_all[:, b * 3072:(b + 1) * 3072]
    maps = [dict(xn1b=np.asarray(rp[c_]["xn1b"]), wupf=wupf, btupc=btupc)
            for c_ in range(8)]
    rfu = run("ffn1up", build_ffn1up, maps)
    maps = []
    for c_ in range(8):
        b, hf = divmod(c_, 2)
        maps.append(dict(h1=np.asarray(rfu[c_]["h1"]), wcnT=wcnT,
                         x1=np.asarray(rp[c_]["x1"]),
                         xn1f=np.asarray(rp[c_]["xn1f"]),
                         dattnf=C(dattn[b][:, hf * TH:(hf + 1) * TH]),
                         dssmf=C(dssm[b][:, hf * TH:(hf + 1) * TH]),
                         bconc=_cols(A(ip["L1_ffn_b_con"])),
                         dbiasc=_cols(A(ip["L1_ffn_down_bias"]))))
    rfd = run("ffn1dn", build_ffn1dn, maps)
    cw2 = A(ip["L2_attn_conv_w"])[:, 0, :]
    maps = []
    for c_ in range(8):
        b, hf = divmod(c_, 2)
        xe = np.zeros((D, 3 + TH), f32)
        xe[:, 3:] = rfd[c_]["xn2"]
        if hf == 1:
            xe[:, 0:3] = rfd[c_ - 1]["xn2"][:, TH - 3:]
        maps.append(dict(
            xn2e=xe,
            cwc2=C(cw2.T.reshape(8, 128, 4).transpose(1, 0, 2).reshape(128, 32)),
            cbc2=_cols(A(ip["L2_attn_conv_b"])),
            wp=A(ip["L2_attn_Wp"]), bpc=A(ip["L2_attn_bp"]).reshape(128, 1),
            ws=A(ip["L2_attn_Ws"]), bsc=A(ip["L2_attn_bs"]).reshape(64, 1),
            gwT=C(A(ip["L2_ffn_gate_w"]).T)))
    r2a = run("l2a", build_l2a, maps)
    btf = [np.concatenate([r2a[2 * b][nm], r2a[2 * b + 1][nm]], axis=1)
           for b in range(B) for nm in ("Bt",)]
    ctf = [np.concatenate([r2a[2 * b]["Ct"], r2a[2 * b + 1]["Ct"]], axis=1)
           for b in range(B)]
    uf = [np.concatenate([r2a[2 * b]["u2"], r2a[2 * b + 1]["u2"]], axis=1)
          for b in range(B)]
    al2 = A(ip["L2_attn_A_log"])
    dl2 = A(ip["L2_attn_dt_log"])
    maps = []
    for c_ in range(8):
        b, nh = divmod(c_, 2)
        rs = slice(nh * 32, (nh + 1) * 32)
        maps.append(dict(Br=C(btf[b][rs]), Cr=C(ctf[b][rs]), ur=C(uf[b][rs]),
                         al32=C(al2[rs].reshape(32, 1)),
                         dl32=C(dl2[rs].reshape(32, 1))))
    r2s = run("l2scan", build_l2scan, maps)
    ylin = [np.concatenate([r2s[2 * b]["y2"], r2s[2 * b + 1]["y2"]], axis=0)
            for b in range(B)]
    RT = np.concatenate([r2a[c_]["Rfm"] for c_ in range(8)], axis=1)
    rth = run("l2thresh", build_l2thresh, [dict(RT=RT) for _ in range(8)])
    thr = rth[0]["thr"]
    w1T = C(np.transpose(A(ip["L2_ffn_w1"]), (0, 2, 1)))
    w2T = C(np.transpose(A(ip["L2_ffn_w2"]), (0, 2, 1)))
    maps = [dict(xn2=np.asarray(rfd[c_]["xn2"]), w1T=w1T, w2T=w2T,
                 Rfm=np.asarray(r2a[c_]["Rfm"]), Pfm=np.asarray(r2a[c_]["Pfm"]),
                 thr=thr) for c_ in range(8)]
    r2m = run("l2moe", build_l2moe, maps)
    maps = []
    for c_ in range(8):
        b, hf = divmod(c_, 2)
        maps.append(dict(x2=np.asarray(rfd[c_]["x2"]), ymoe=np.asarray(r2m[c_]["ymoe"]),
                         ylin=C(ylin[b][:, hf * TH:(hf + 1) * TH]),
                         wo2=A(ip["L2_attn_Wo"]), boc2=_cols(A(ip["L2_attn_bo"])),
                         wnc=_cols(A(ip["L3_attn_w_norm"]))))
    rc2 = run("comb2", build_comb2, maps)
    maps = [dict(xrmsb=np.asarray(rc2[c_]["xrmsb"]), win=A(ip["L3_attn_W_in"]),
                 lamc=_cols(A(ip["L3_attn_lambda_log"]))) for c_ in range(8)]
    r3a = run("l3a", build_l3a, maps)
    afm = [np.concatenate([r3a[2 * b]["afm"], r3a[2 * b + 1]["afm"]], axis=1)
           for b in range(B)]
    ufm = [np.concatenate([r3a[2 * b]["ufm"], r3a[2 * b + 1]["ufm"]], axis=1)
           for b in range(B)]
    maps = []
    for c_ in range(8):
        rs = slice(c_ * 128, (c_ + 1) * 128)
        maps.append(dict(asl=C(np.concatenate([afm[b][rs] for b in range(B)], axis=1)),
                         usl=C(np.concatenate([ufm[b][rs] for b in range(B)], axis=1))))
    r3s = run("l3scan", build_l3scan, maps)
    yfm = []
    for b in range(B):
        yb = np.concatenate([r3s[c_]["y3"][:, b * T:(b + 1) * T] for c_ in range(8)],
                            axis=0)
        yfm.append(yb)
    ewT = C(np.transpose(A(ip["L3_ffn_expert_w"]), (0, 2, 1)))
    eb3 = A(ip["L3_ffn_expert_b"])
    ebc = np.concatenate([C(eb3[e].reshape(8, 128).T) for e in range(8)], axis=1)
    maps = []
    for c_ in range(8):
        b, hf = divmod(c_, 2)
        maps.append(dict(x3=np.asarray(rc2[c_]["x3"]), xn3=np.asarray(rc2[c_]["xn3"]),
                         yfm=C(yfm[b][:, hf * TH:(hf + 1) * TH]),
                         wout=A(ip["L3_attn_W_out"]),
                         gwT3=C(A(ip["L3_ffn_gate_w"]).T),
                         gbrep=C(np.tile(A(ip["L3_ffn_gate_b"]), (128, 1))),
                         ewT=ewT, ebc=ebc))
    r3f = run("l3final", build_l3final, maps)
    out = np.zeros((B, T, D), f32)
    for c_ in range(8):
        b, hf = divmod(c_, 2)
        out[b, hf * TH:(hf + 1) * TH, :] = r3f[c_]["outf"].T
    return out


def kernel(**inputs):
    r0 = _l0(inputs)
    return _l123(inputs, r0)
